# version 05 (final)
# baseline (speedup 1.0000x reference)
import numpy as np
import ml_dtypes
import concourse.bass as bass
import concourse.mybir as mybir
from concourse.bass_utils import run_bass_kernel_spmd

F32 = mybir.dt.float32
BF16 = mybir.dt.bfloat16
I32 = mybir.dt.int32
AF = mybir.ActivationFunctionType
ALU = mybir.AluOpType
AX = mybir.AxisListType

B, T, C, H, HD = 2, 8192, 2048, 16, 128
CC = C // 128
INTER = 5504
NI = INTER // 128
SCALE = float(1.0 / np.sqrt(HD))
MB = -30.0
EPS = 1e-5
BOUNDS = (0, 3072, 5248, 6912, 8192)
NMAX = 3072


def cdiv(a, b):
    return (a + b - 1) // b


class Ctx:
    def __init__(self, E, eng, t, sems, ctr=None):
        self.E = E
        self.eng = eng
        self.t = t
        self.sems = sems
        self.ctr = dict(ctr) if ctr else {k: 0 for k in sems}

    def step(self, owner, emit, incs=()):
        if self.E == owner:
            ins = emit(self.eng)
            for s, n in incs:
                ins = ins.then_inc(self.sems[s], n)
        for s, n in incs:
            self.ctr[s] += n

    def wait(self, owner, sem, val=None):
        if val is None:
            val = self.ctr[sem]
        if val <= 0:
            return
        if self.E == owner:
            self.eng.wait_ge(self.sems[sem], val)

    def drain(self, owner):
        if self.E == owner:
            self.eng.drain()



def emit_prologue(cx):
    t = cx.t
    for dst, src in (
        (t["sb_lut"], t["lut"]),
        (t["sb_ones_f"], t["ones_f"]),
        (t["sb_ones_w"], t["ones_w"]),
        (t["sb_onescol"], t["onescol"]),
        (t["sb_identf"], t["identf"]),
        (t["sb_dmask"], t["dmask"].rearrange("d p w -> p d w")),
        (t["sb_selh"], t["selh"]),
        (t["sb_mb"], t["mbias"]),
        (t["sb_eps"], t["epsv"]),
        (t["sb_zero"], t["zerov"]),
    ):
        cx.step("sync", lambda e, d=dst, s=src: e.dma_start(out=d, in_=s), [("ld0", 16)])
    cx.step("gpsimd", lambda e: e.dma_start(out=t["sb_battn"], in_=t["battn"]), [("ld0", 16)])
    cx.step("gpsimd", lambda e: e.dma_start(out=t["sb_bproj"], in_=t["bproj"]), [("ld0", 16)])
    cx.wait("tensor", "ld0")
    cx.step("tensor", lambda e: e.matmul(t["ps"][6][0:128, 0:16], t["sb_ones_f"],
                                         t["sb_lut"], start=True, stop=True), [("pe0", 1)])
    cx.wait("vector", "pe0")
    cx.step("vector", lambda e: e.tensor_copy(t["sb_lutbc"], t["ps"][6][0:128, 0:16]),
            [("ve0", 1)])


def gather_jobs():
    jobs = []
    for (srcn, dstn, R, W) in (("didx", "wd_d", C, INTER),
                               ("gidx", "wg_d", INTER, C),
                               ("uidx", "wu_d", INTER, C)):
        for r0 in range(0, R, 128):
            jobs.append((srcn, dstn, r0, W))
    return jobs


def emit_gather(cx, seg):
    t = cx.t
    jobs = gather_jobs()
    jobs = jobs[:40] if seg == 0 else jobs[40:]
    cx.wait("vector", "ve0", 1)
    for (srcn, dstn, r0, W) in jobs:
        src, dst = t[srcn], t[dstn]
        jn = cx.ctr["gld"] // 16
        sl = jn % 2
        cx.wait("gpsimd", "gst", max(0, cx.ctr["gst"] - 16))
        cx.step("gpsimd", lambda e, s=src, a=r0, w=W, q=sl: e.dma_start(
            out=t["sb_idxb"][q][:, 0:w], in_=s[a:a + 128, :]), [("gld", 16)])
        cx.wait("vector", "gld")
        cx.wait("vector", "gst", max(0, cx.ctr["gst"] - 16))
        for k in range(16):
            dd = t["sb_gacc"][sl] if k == 0 else t["sb_gtmp"][sl]
            cx.step("vector", lambda e, w=W, q=sl, kk=k, d2=dd: e.tensor_scalar(
                out=d2[:, 0:w], in0=t["sb_idxb"][q][:, 0:w],
                scalar1=float(kk), scalar2=t["sb_lutbc"][:, kk:kk + 1],
                op0=ALU.is_equal, op1=ALU.mult))
            if k > 0:
                cx.step("vector", lambda e, w=W, q=sl: e.tensor_tensor(
                    out=t["sb_gacc"][q][:, 0:w], in0=t["sb_gacc"][q][:, 0:w],
                    in1=t["sb_gtmp"][q][:, 0:w], op=ALU.add),
                    [("gdone", 1)] if k == 15 else ())
        cx.wait("sync", "gdone")
        cx.step("sync", lambda e, d=dst, a=r0, w=W, q=sl: e.dma_start(
            out=d[a:a + 128, :], in_=t["sb_gacc"][q][:, 0:w]), [("gst", 16)])


def emit_wtrans(cx):
    t = cx.t
    cx.wait("scalar", "gst")
    jobs = []
    for (srcn, dstn) in (("wg_d", "wgT_d"), ("wu_d", "wuT_d")):
        for c0 in range(0, C, 128):
            for i0 in range(0, INTER, 512):
                iw = min(512, INTER - i0)
                jobs.append((srcn, dstn, i0, c0, iw, 128, c0, i0))
    for i0 in range(0, INTER, 128):
        for c0 in range(0, C, 512):
            jobs.append(("wd_d", "wdT_d", c0, i0, 512, 128, i0, c0))
    for (srcn, dstn, r0, c0, rh, cw, dr, dc) in jobs:
        src, dst = t[srcn], t[dstn]
        jn = cx.ctr["wtld"] // 16
        sl = jn % 2
        cx.wait("scalar", "wtst", max(0, cx.ctr["wtst"] - 16))
        cx.step("scalar", lambda e, s=src, a=r0, b=c0, hh=rh, w=cw, q=sl: e.dma_start(
            out=t["sb_wt"][q][0:w, 0:hh], in_=s[a:a + hh, b:b + w], transpose=True),
            [("wtld", 16)])
        cx.wait("scalar", "wtld")
        cx.step("scalar", lambda e, d=dst, a=dr, b=dc, hh=rh, w=cw, q=sl: e.dma_start(
            out=d[a:a + w, b:b + hh], in_=t["sb_wt"][q][0:w, 0:hh]), [("wtst", 16)])


def emit_xprep(cx, beta, P):
    t = cx.t
    cx.step("gpsimd", lambda e, bb=beta, pp=P: e.dma_start(
        out=t["xbf_d"][0:pp, :], in_=t["x"][bb:bb + 1, 0:pp, :].squeeze()), [("xc", 16)])
    cx.wait("sync", "xc")
    for tt in range(cdiv(P, 512)):
        w = min(512, P - tt * 512)
        for ccb in range(CC):
            sl = (tt * CC + ccb) % 2
            cx.wait("sync", "xtst", max(0, cx.ctr["xtst"] - 16))
            cx.step("sync", lambda e, a=tt * 512, b=ccb * 128, ww=w, q=sl: e.dma_start(
                out=t["sb_xtr"][q][0:128, 0:ww], in_=t["xbf_d"][a:a + ww, b:b + 128],
                transpose=True), [("xtld", 16)])
            cx.wait("sync", "xtld")
            cx.step("sync", lambda e, a=ccb * 128, b=tt * 512, ww=w, q=sl: e.dma_start(
                out=t["xT_d"][a:a + 128, b:b + ww], in_=t["sb_xtr"][q][0:128, 0:ww]),
                [("xtst", 16)])


def emit_qkv(cx, beta, T0, T1, P):
    t = cx.t
    wslab = t["v_wslab"]
    xts = t["v_xts"]
    kev = t["v_kev"]
    vev = t["v_vev"]
    for which, colbase, tok0, tok1, dstn in (("q", 0, T0, T1, "qT_d"),
                                             ("k", C, 0, P, "kT_d")):
        dstd = t[dstn]
        for ch2 in range(2):
            cb = colbase + ch2 * 1024
            cx.wait("gpsimd", "kmm", cx.ctr["kmm"])
            cx.wait("gpsimd", "vmm", cx.ctr["vmm"])
            cx.step("gpsimd", lambda e, b=cb: e.dma_start(
                out=wslab,
                in_=t["wattn"].rearrange("(cc p) d -> p cc d", p=128)[:, :, b:b + 1024]),
                [("wsl", 16)])
            cx.wait("tensor", "wsl")
            ntt = cdiv(tok1 - tok0, 512)
            for tt in range(ntt):
                a = tok0 + tt * 512
                w = min(512, tok1 - a)
                cx.wait("sync", "kmm", cx.ctr["kmm"])
                cx.step("sync", lambda e, aa=a, ww=w: e.dma_start(
                    out=xts[:, :, 0:ww],
                    in_=t["xT_d"].rearrange("(cc p) tok -> p cc tok", p=128)[:, :, aa:aa + ww]),
                    [("xsl", 16)])
                cx.wait("tensor", "xsl")
                for dc in range(8):
                    bank = t["ps"][dc % 2]
                    cx.wait("tensor", "kev", max(0, cx.ctr["kev"] - 1))
                    for c2 in range(CC):
                        cx.step("tensor", lambda e, cc2=c2, d=dc, ww=w, bk=bank: e.matmul(
                            bk[:, 0:ww], wslab[:, cc2, d * 128:(d + 1) * 128],
                            xts[:, cc2, 0:ww], start=(cc2 == 0), stop=False))
                    cx.step("tensor", lambda e, d=dc, b2=cb, ww=w, bk=bank: e.matmul(
                        bk[:, 0:ww], t["sb_battn"][0:1, b2 + d * 128:b2 + (d + 1) * 128],
                        t["sb_ones_w"][0:1, 0:ww], start=False, stop=True), [("kmm", 1)])
                    cx.wait("scalar", "kmm")
                    ev = kev[dc % 3]
                    cx.wait("scalar", "kst", max(0, cx.ctr["kst"] - 32))
                    cx.step("scalar", lambda e, ww=w, bk=bank, v=ev: e.copy(
                        out=v[:, 0:ww], in_=bk[:, 0:ww]), [("kev", 1)])
                    cx.wait("sync", "kev")
                    row = ch2 * 1024 + dc * 128
                    aa = (a - tok0) if which == "q" else a
                    cx.step("sync", lambda e, r=row, a2=aa, ww=w, v=ev, dd=dstd: e.dma_start(
                        out=dd[r:r + 128, a2:a2 + ww], in_=v[:, 0:ww]), [("kst", 16)])
    for ch2 in range(2):
        cb = 2 * C + ch2 * 1024
        cx.wait("gpsimd", "kmm", cx.ctr["kmm"])
        cx.wait("gpsimd", "vmm", cx.ctr["vmm"])
        cx.step("gpsimd", lambda e, b=cb: e.dma_start(
            out=wslab,
            in_=t["wattn"].rearrange("(cc p) d -> p cc d", p=128)[:, :, b:b + 1024]),
            [("wsl", 16)])
        cx.wait("tensor", "wsl")
        ntt = cdiv(P, 512)
        for tt in range(ntt):
            a = tt * 512
            w = min(512, P - a)
            cx.wait("sync", "vmm", cx.ctr["vmm"])
            cx.step("sync", lambda e, aa=a, ww=w: e.dma_start(
                out=xts[:, :, 0:ww],
                in_=t["xT_d"].rearrange("(cc p) tok -> p cc tok", p=128)[:, :, aa:aa + ww]),
                [("xsl", 16)])
            cx.wait("tensor", "xsl")
            for tk in range(cdiv(w, 128)):
                wk = min(128, w - tk * 128)
                for vc in range(2):
                    bank = t["ps"][4 + vc]
                    cx.wait("tensor", "vev", cx.ctr["vev"])
                    for c2 in range(CC):
                        cx.step("tensor", lambda e, cc2=c2, k=tk, wkk=wk, v=vc, bk=bank: e.matmul(
                            bk[0:wkk, 0:512], xts[:, cc2, k * 128:k * 128 + wkk],
                            wslab[:, cc2, v * 512:(v + 1) * 512],
                            start=(cc2 == 0), stop=False))
                    cx.step("tensor", lambda e, v=vc, b2=cb, wkk=wk, bk=bank: e.matmul(
                        bk[0:wkk, 0:512], t["sb_ones_w"][0:1, 0:wkk],
                        t["sb_battn"][0:1, b2 + v * 512:b2 + (v + 1) * 512],
                        start=False, stop=True), [("vmm", 1)])
                cx.wait("scalar", "vmm")
                vv = vev[tk % 2]
                cx.wait("scalar", "vst", max(0, cx.ctr["vst"] - 16))
                for vc in range(2):
                    cx.step("scalar", lambda e, v=vc, wkk=wk, v2=vv: e.copy(
                        out=v2[0:wkk, v * 512:(v + 1) * 512],
                        in_=t["ps"][4 + v][0:wkk, 0:512]), [("vev", 1)])
                cx.wait("sync", "vev")
                cx.step("sync", lambda e, aa=a + tk * 128, wkk=wk, v2=vv, b2=ch2: e.dma_start(
                    out=t["v_d"][aa:aa + wkk, b2 * 1024:(b2 + 1) * 1024],
                    in_=v2[0:wkk, :]), [("vst", 16)])


def emit_attention(cx, beta, T0, T1, P):
    t = cx.t
    n = T1 - T0
    nqg = cdiv(n, 512)
    Kh = t["v_K"]
    Vh = t["v_V"]
    qtl = t["v_q"]
    PT = t["v_PT"]
    yev = t["v_yev"]
    cx.wait("tensor", "kev", cx.ctr["kev"])
    cx.wait("tensor", "vev", cx.ctr["vev"])
    cx.wait("sync", "kmm", cx.ctr["kmm"])
    cx.wait("sync", "vmm", cx.ctr["vmm"])
    cx.wait("sync", "kev", cx.ctr["kev"])
    cx.wait("sync", "vev", cx.ctr["vev"])
    for h in range(H):
        cx.wait("sync", "summ", cx.ctr["summ"])
        cx.step("sync", lambda e, hh=h, pp=P: e.dma_start(
            out=Kh[:, 0:pp], in_=t["kT_d"][hh * 128:(hh + 1) * 128, 0:pp]),
            [("kvld", 16)])
        ntka = cdiv(P, 128)
        cx.step("sync", lambda e, hh=h, nt=ntka: e.dma_start(
            out=Vh[:, 0:nt, :],
            in_=t["v_d"].rearrange("(ck p) d -> p ck d", p=128)[:, 0:nt, hh * 128:(hh + 1) * 128]),
            [("kvld", 16)])
        cx.wait("tensor", "kvld")
        for qg in range(nqg):
            q0 = qg * 512
            w = min(512, n - q0)
            qs = T0 + q0
            nfull = qs // 128
            ndiag = cdiv(w, 128)
            nch = nfull + ndiag
            cx.wait("sync", "summ", max(0, cx.ctr["summ"] - 1))
            cx.step("sync", lambda e, hh=h, a=q0, ww=w, q=qg % 2: e.dma_start(
                out=qtl[q][:, 0:ww], in_=t["qT_d"][hh * 128:(hh + 1) * 128, a:a + ww]),
                [("qld", 16)])
            cx.wait("tensor", "qld")
            ybank = t["ps"][2 + (qg % 2)]
            smbank = t["ps"][4 + (qg % 2)]
            cx.wait("tensor", "yev", max(0, cx.ctr["yev"] - 1))
            cx.wait("tensor", "smev", max(0, cx.ctr["smev"] - 1))
            for ch in range(nch):
                diag = ch >= nfull
                kw = 128 if not diag else min(128, qs + w - ch * 128)
                sb = t["ps"][ch % 2]
                pt = PT[ch % 4]
                cx.wait("tensor", "pexp", max(0, cx.ctr["pexp"] - 1))
                cx.step("tensor", lambda e, a=ch * 128, kk=kw, ww=w, g=qg % 2, bk=sb: e.matmul(
                    bk[0:kk, 0:ww], Kh[:, a:a + kk], qtl[g][:, 0:ww],
                    start=True, stop=True), [("smm", 1)])
                cx.wait("scalar", "smm")
                cx.wait("scalar", "pfree", max(0, cx.ctr["pfree"] - 2))
                cx.step("scalar", lambda e, kk=kw, ww=w, bk=sb, p=pt: e.activation(
                    out=p[0:kk, 0:ww], in_=bk[0:kk, 0:ww], func=AF.Exp,
                    bias=t["sb_mb"], scale=SCALE), [("pexp", 1)])
                if diag:
                    d = ch - nfull
                    cx.wait("gpsimd", "pexp")
                    cx.step("gpsimd", lambda e, kk=kw, ww=w, p=pt, dd=d: e.tensor_tensor(
                        out=p[0:kk, 0:ww], in0=p[0:kk, 0:ww],
                        in1=t["sb_dmask"][0:kk, dd, 0:ww], op=ALU.mult), [("pmsk", 1)])
                    cx.wait("tensor", "pmsk")
                else:
                    cx.wait("tensor", "pexp")
                cx.step("tensor", lambda e, a=ch, kk=kw, ww=w, bk=ybank, p=pt: e.matmul(
                    bk[:, 0:ww], Vh[0:kk, a, :], p[0:kk, 0:ww],
                    start=(ch == 0), stop=(ch == nch - 1)),
                    [("ymm", 1)] if ch == nch - 1 else ())
                cx.step("tensor", lambda e, kk=kw, ww=w, bk=smbank, p=pt: e.matmul(
                    bk[0:1, 0:ww], t["sb_onescol"][0:kk, 0:1], p[0:kk, 0:ww],
                    start=(ch == 0), stop=(ch == nch - 1)), [("pfree", 1)])
                if ch == nch - 1:
                    cx.step("tensor", lambda e: e.nop(), [("summ", 1)])
            cx.wait("scalar", "ymm")
            cx.wait("scalar", "yst", max(0, cx.ctr["yst"] - 32))
            yv = yev[qg % 2]
            sv = t["sb_smev"][qg % 2]
            cx.step("scalar", lambda e, ww=w, bk=ybank, v=yv: e.copy(
                out=v[:, 0:ww], in_=bk[:, 0:ww]), [("yev", 1)])
            cx.step("scalar", lambda e, ww=w, bk=smbank, v=sv: e.copy(
                out=v[0:1, 0:ww], in_=bk[0:1, 0:ww]), [("smev", 1)])
            cx.wait("sync", "yev")
            cx.step("sync", lambda e, hh=h, a=q0, ww=w, v=yv: e.dma_start(
                out=t["y_d"][hh * 128:(hh + 1) * 128, a:a + ww], in_=v[:, 0:ww]),
                [("yst", 16)])
            cx.wait("sync", "smev")
            cx.step("sync", lambda e, hh=h, a=q0, ww=w, v=sv: e.dma_start(
                out=t["sums_d"][hh:hh + 1, a:a + ww], in_=v[0:1, 0:ww]), [("yst", 16)])


def emit_proj_ln1(cx, beta, T0, T1):
    t = cx.t
    n = T1 - T0
    wp = t["v_wp"]
    ysl = t["v_ysl"]
    ysc = t["v_ysc"]
    zb = t["v_zb"]
    zsqb = t["v_zsq"]
    x1bb = t["v_x1bb"]
    cx.wait("sync", "summ", cx.ctr["summ"])
    cx.wait("sync", "yev", cx.ctr["yev"])
    cx.wait("tensor", "pexp", cx.ctr["pexp"])
    cx.wait("tensor", "pmsk", cx.ctr["pmsk"])
    cx.wait("tensor", "yev", cx.ctr["yev"])
    cx.wait("tensor", "smev", cx.ctr["smev"])
    ntt = cdiv(n, 512)
    for tt in range(ntt):
        a = tt * 512
        w = min(512, n - a)
        cx.wait("sync", "x1b", cx.ctr["x1b"])
        cx.step("sync", lambda e, aa=a, ww=w: e.dma_start(
            out=ysl[:, :, 0:ww],
            in_=t["y_d"].rearrange("(hh p) tok -> p hh tok", p=128)[:, :, aa:aa + ww]),
            [("ysl", 16)])
        cx.step("sync", lambda e, aa=a, ww=w: e.dma_start(
            out=t["sb_sums16"][0:H, 0:ww], in_=t["sums_d"][:, aa:aa + ww]), [("ysl", 16)])
        cx.wait("vector", "ysl")
        cx.step("vector", lambda e, ww=w: e.reciprocal(
            t["sb_rec16"][0:H, 0:ww], t["sb_sums16"][0:H, 0:ww]))
        cx.drain("vector")
        cx.step("vector", lambda e, ww=w: e.tensor_copy(
            t["sb_rec16b"][0:H, 0:ww], t["sb_rec16"][0:H, 0:ww]), [("rec", 1)])
        for hh in range(H):
            cx.wait("tensor", "rec", cx.ctr["rec"])
            cx.wait("tensor", "ysc", cx.ctr["ysc"])
            cx.step("tensor", lambda e, h2=hh, ww=w: e.matmul(
                t["ps"][6][:, 0:ww], t["sb_selh"][0:H, h2 * 128:(h2 + 1) * 128],
                t["sb_rec16b"][0:H, 0:ww], start=True, stop=True), [("bcm", 1)])
            cx.wait("scalar", "bcm")
            cx.step("scalar", lambda e, h2=hh, ww=w: e.copy(
                out=ysc[:, h2, 0:ww], in_=t["ps"][6][:, 0:ww]), [("ysc", 1)])
        cx.wait("vector", "ysc", cx.ctr["ysc"])
        if not DBG_NO_YSC:
            cx.step("vector", lambda e, ww=w: e.tensor_tensor(
                out=ysl[:, :, 0:ww], in0=ysl[:, :, 0:ww], in1=ysc[:, :, 0:ww],
                op=ALU.mult), [("yscal", 1)])
        else:
            cx.step("vector", lambda e: e.nop(), [("yscal", 1)])
        cx.wait("tensor", "yscal", cx.ctr["yscal"])
        for cq in range(4):
            cx.wait("gpsimd", "pmm", cx.ctr["pmm"])
            cx.step("gpsimd", lambda e, q2=cq: e.dma_start(
                out=wp,
                in_=t["wproj"].rearrange("(cc p) d -> p cc d", p=128)[:, :, q2 * 512:(q2 + 1) * 512]),
                [("wpld", 16)])
            cx.wait("tensor", "wpld")
            for c4 in range(4):
                co = cq * 4 + c4
                bank = t["ps"][co % 2]
                for hh in range(H):
                    cx.step("tensor", lambda e, h2=hh, c2=c4, ww=w, bk=bank: e.matmul(
                        bk[:, 0:ww], wp[:, h2, c2 * 128:(c2 + 1) * 128],
                        ysl[:, h2, 0:ww], start=(h2 == 0), stop=False))
                cx.step("tensor", lambda e, c2=co, ww=w, bk=bank: e.matmul(
                    bk[:, 0:ww], t["sb_bproj"][0:1, c2 * 128:(c2 + 1) * 128],
                    t["sb_ones_w"][0:1, 0:ww], start=False, stop=True), [("pmm", 1)])
                for bi in range(cdiv(w, 128)):
                    bw = min(128, w - bi * 128)
                    slot = (cx.ctr["xblk"] // 16) % 2
                    cx.wait("sync", "xtr", cx.ctr["xtr"])
                    cx.step("sync", lambda e, aa=T0 + a + bi * 128, c2=co, bb=bw, q=slot: e.dma_start(
                        out=t["sb_xblk"][q][0:bb, :],
                        in_=t["x"][beta:beta + 1, aa:aa + bb, c2 * 128:(c2 + 1) * 128].squeeze()),
                        [("xblk", 16)])
                    cx.wait("tensor", "xblk")
                    cx.wait("tensor", "xtoe", cx.ctr["xtoe"])
                    cx.wait("tensor", "x1tmev", cx.ctr["x1tmev"])
                    cx.step("tensor", lambda e, bb=bw, q=slot: e.transpose(
                        t["ps"][3][:, 0:bb], t["sb_xblk"][q][0:bb, :], t["sb_identf"]),
                        [("xtr", 1)])
                    cx.wait("vector", "xtr")
                    cx.step("vector", lambda e, bb=bw, b2=bi: e.tensor_copy(
                        t["sb_xto"][:, b2 * 128:b2 * 128 + bb], t["ps"][3][:, 0:bb]),
                        [("xtoe", 1)])
                cx.wait("vector", "pmm")
                cx.wait("vector", "zsq", cx.ctr["zsq"])
                if not DBG_NO_RESID:
                    cx.step("vector", lambda e, ww=w, bk=bank: e.tensor_tensor(
                        out=t["sb_z"][:, 0:ww], in0=bk[:, 0:ww], in1=t["sb_xto"][:, 0:ww],
                        op=ALU.add), [("zf", 1)])
                else:
                    cx.step("vector", lambda e, ww=w, bk=bank: e.tensor_scalar(
                        out=t["sb_z"][:, 0:ww], in0=bk[:, 0:ww], scalar1=1.0,
                        scalar2=None, op0=ALU.mult), [("zf", 1)])
                cx.step("vector", lambda e, c2=co, ww=w: e.tensor_copy(
                    zb[:, c2, 0:ww], t["sb_z"][:, 0:ww]), [("zev", 1)])
                cx.wait("scalar", "zf")
                cx.step("scalar", lambda e, ww=w: e.activation(
                    out=zsqb[:, 0:ww], in_=t["sb_z"][:, 0:ww], func=AF.Square),
                    [("zsq", 1)])
                cx.wait("tensor", "zsq")
                cx.wait("tensor", "zev", cx.ctr["zev"])
                cx.step("tensor", lambda e, c2=co, ww=w: e.matmul(
                    t["ps"][4][0:1, 0:ww], t["sb_onescol"], zb[:, c2, 0:ww],
                    start=(c2 == 0), stop=(c2 == CC - 1)),
                    [("stm", 1)] if co == CC - 1 else ())
                cx.step("tensor", lambda e, c2=co, ww=w: e.matmul(
                    t["ps"][5][0:1, 0:ww], t["sb_onescol"], zsqb[:, 0:ww],
                    start=(c2 == 0), stop=(c2 == CC - 1)),
                    [("stm", 1)] if co == CC - 1 else ())
        cx.wait("scalar", "stm")
        cx.step("scalar", lambda e, ww=w: e.activation(
            out=t["sb_mu"][0:1, 0:ww], in_=t["ps"][4][0:1, 0:ww], func=AF.Copy,
            bias=0.0, scale=1.0 / C), [("mu", 1)])
        cx.step("scalar", lambda e, ww=w: e.activation(
            out=t["sb_ex2"][0:1, 0:ww], in_=t["ps"][5][0:1, 0:ww], func=AF.Copy,
            bias=0.0, scale=1.0 / C), [("mu", 1)])
        cx.step("scalar", lambda e, ww=w: e.activation(
            out=t["sb_mu2"][0:1, 0:ww], in_=t["sb_mu"][0:1, 0:ww], func=AF.Square),
            [("mu", 1)])
        cx.wait("vector", "mu")
        cx.step("vector", lambda e, ww=w: e.tensor_tensor(
            out=t["sb_ex2"][0:1, 0:ww], in0=t["sb_ex2"][0:1, 0:ww],
            in1=t["sb_mu2"][0:1, 0:ww], op=ALU.subtract), [("var", 1)])
        cx.wait("scalar", "var")
        cx.step("scalar", lambda e, ww=w: e.activation(
            out=t["sb_mu2"][0:1, 0:ww], in_=t["sb_ex2"][0:1, 0:ww], func=AF.Ln,
            bias=t["sb_eps"][0:1, :], scale=1.0), [("lnv", 1)])
        cx.step("scalar", lambda e, ww=w: e.activation(
            out=t["sb_ex2"][0:1, 0:ww], in_=t["sb_mu2"][0:1, 0:ww], func=AF.Exp,
            scale=-0.5, bias=t["sb_zero"][0:1, :]), [("lnv", 1)])
        cx.wait("tensor", "lnv")
        cx.step("tensor", lambda e, ww=w: e.matmul(
            t["ps"][4][:, 0:ww], t["sb_ones_f"], t["sb_mu"][0:1, 0:ww],
            start=True, stop=True), [("bc2", 1)])
        cx.step("tensor", lambda e, ww=w: e.matmul(
            t["ps"][5][:, 0:ww], t["sb_ones_f"], t["sb_ex2"][0:1, 0:ww],
            start=True, stop=True), [("bc2", 1)])
        cx.wait("scalar", "bc2")
        cx.step("scalar", lambda e, ww=w: e.copy(
            out=t["sb_mub"][:, 0:ww], in_=t["ps"][4][:, 0:ww]), [("bc2e", 1)])
        cx.step("scalar", lambda e, ww=w: e.copy(
            out=t["sb_rstdb"][:, 0:ww], in_=t["ps"][5][:, 0:ww]), [("bc2e", 1)])
        cx.wait("vector", "bc2e")
        for co in range(CC):
            cx.wait("vector", "x1b", cx.ctr["x1b"])
            cx.wait("vector", "x1tr", cx.ctr["x1tr"])
            cx.step("vector", lambda e, c2=co, ww=w: e.tensor_tensor(
                out=t["sb_x1f"][:, 0:ww], in0=zb[:, c2, 0:ww],
                in1=t["sb_mub"][:, 0:ww], op=ALU.subtract))
            cx.step("vector", lambda e, ww=w: e.tensor_tensor(
                out=t["sb_x1f"][:, 0:ww], in0=t["sb_x1f"][:, 0:ww],
                in1=t["sb_rstdb"][:, 0:ww], op=ALU.mult), [("x1f", 1)])
            cx.wait("scalar", "x1f")
            cx.wait("scalar", "x1bst", max(0, cx.ctr["x1bst"] - 16))
            xbslot = x1bb[co % 2]
            cx.step("scalar", lambda e, ww=w, v=xbslot: e.copy(
                out=v[:, 0:ww], in_=t["sb_x1f"][:, 0:ww]), [("x1b", 1)])
            cx.wait("sync", "x1b")
            cx.step("sync", lambda e, aa=a, c2=co, ww=w, v=xbslot: e.dma_start(
                out=t["x1T_d"][c2 * 128:(c2 + 1) * 128, aa:aa + ww],
                in_=v[:, 0:ww]), [("x1bst", 16)])
            for bi in range(cdiv(w, 128)):
                bw = min(128, w - bi * 128)
                tms = cx.ctr["x1tmev"] % 2
                cx.wait("tensor", "x1f", cx.ctr["x1f"])
                cx.wait("tensor", "x1tmev", cx.ctr["x1tmev"])
                cx.step("tensor", lambda e, b2=bi, bb=bw: e.transpose(
                    t["ps"][3][0:bb, 0:128], t["sb_x1f"][:, b2 * 128:b2 * 128 + bb],
                    t["sb_identf"]), [("x1tr", 1)])
                cx.wait("vector", "x1tr")
                cx.wait("vector", "x1st", max(0, cx.ctr["x1st"] - 16))
                cx.step("vector", lambda e, bb=bw, q=tms: e.tensor_copy(
                    t["sb_x1tm"][q][0:bb, :], t["ps"][3][0:bb, 0:128]), [("x1tmev", 1)])
                cx.wait("sync", "x1tmev")
                cx.step("sync", lambda e, aa=a + bi * 128, c2=co, bb=bw, q=tms: e.dma_start(
                    out=t["x1tm_d"][aa:aa + bb, c2 * 128:(c2 + 1) * 128],
                    in_=t["sb_x1tm"][q][0:bb, :]), [("x1st", 16)])


def emit_ffn_ln2(cx, beta, T0, T1):
    t = cx.t
    n = T1 - T0
    ntt = cdiv(n, 512)
    x1T = t["mk_bf"](0, CC * n).rearrange("p (cc tok) -> p cc tok", cc=CC)
    SW = 512
    wg_sl = t["mk_bf"](49408 * 2, CC * SW).rearrange("p (cc i) -> p cc i", cc=CC)
    wu_sl = t["mk_bf"]((49408 + CC * SW) * 2, CC * SW).rearrange("p (cc i) -> p cc i", cc=CC)
    cx.wait("sync", "x1bst", cx.ctr["x1bst"])
    cx.wait("sync", "x1st", cx.ctr["x1st"])
    cx.wait("sync", "wtst", cx.ctr["wtst"])
    cx.wait("tensor", "zf", cx.ctr["zf"])
    cx.wait("tensor", "x1tmev", cx.ctr["x1tmev"])
    cx.wait("tensor", "bc2e", cx.ctr["bc2e"])
    cx.step("sync", lambda e: e.dma_start(
        out=x1T[:, :, 0:n],
        in_=t["x1T_d"].rearrange("(cc p) tok -> p cc tok", p=128)[:, :, 0:n]),
        [("x1Tld", 16)])
    cx.wait("tensor", "x1Tld")
    nq = cdiv(INTER, SW)
    for iq in range(nq):
        i0 = iq * SW
        iw = min(SW, INTER - i0)
        cx.wait("sync", "gumm", cx.ctr["gumm"])
        for (wtn, sl2) in (("wgT_d", wg_sl), ("wuT_d", wu_sl)):
            cx.step("sync", lambda e, s=t[wtn], a=i0, ww=iw, d=sl2: e.dma_start(
                out=d[:, :, 0:ww],
                in_=s.rearrange("(cc p) i -> p cc i", p=128)[:, :, a:a + ww]),
                [("wqld", 16)])
        cx.wait("tensor", "wqld")
        for it in range(cdiv(iw, 128)):
            ib = min(128, iw - it * 128)
            for tt in range(ntt):
                w = min(512, n - tt * 512)
                par = cx.ctr["gumm"] % 2
                gbank = t["ps"][0 + par]
                ubank = t["ps"][2 + par]
                cx.wait("tensor", "hev", max(0, cx.ctr["hev"] - 2))
                for c2 in range(CC):
                    cx.step("tensor", lambda e, cc2=c2, i2=it, ww=w, t2=tt, ii=ib, bk=gbank: e.matmul(
                        bk[0:ii, 0:ww], wg_sl[:, cc2, i2 * 128:i2 * 128 + ii],
                        x1T[:, cc2, t2 * 512:t2 * 512 + ww],
                        start=(cc2 == 0), stop=(cc2 == CC - 1)))
                for c2 in range(CC):
                    cx.step("tensor", lambda e, cc2=c2, i2=it, ww=w, t2=tt, ii=ib, bk=ubank: e.matmul(
                        bk[0:ii, 0:ww], wu_sl[:, cc2, i2 * 128:i2 * 128 + ii],
                        x1T[:, cc2, t2 * 512:t2 * 512 + ww],
                        start=(cc2 == 0), stop=(cc2 == CC - 1)),
                        [("gumm", 1)] if c2 == CC - 1 else ())
                cx.wait("scalar", "gumm")
                cx.step("scalar", lambda e, ww=w, ii=ib, bk=gbank: e.activation(
                    out=t["sb_sil"][0:ii, 0:ww], in_=bk[0:ii, 0:ww], func=AF.Silu),
                    [("sil", 1)])
                cx.wait("vector", "sil")
                cx.wait("vector", "hst", max(0, cx.ctr["hst"] - 32))
                hv = t["sb_hev"][cx.ctr["hev"] % 2]
                cx.step("vector", lambda e, ww=w, ii=ib, bk=ubank, v=hv: e.tensor_tensor(
                    out=v[0:ii, 0:ww], in0=t["sb_sil"][0:ii, 0:ww], in1=bk[0:ii, 0:ww],
                    op=ALU.mult), [("hev", 1)])
                cx.wait("sync", "hev")
                cx.step("sync", lambda e, a2=i0 + it * 128, b2=tt * 512, ww=w, ii=ib, v=hv: e.dma_start(
                    out=t["h_d"][a2:a2 + ii, b2:b2 + ww], in_=v[0:ii, 0:ww]), [("hst", 16)])
    wdh = t["mk_bf"](0, NI * 1024).rearrange("p (ni c2) -> p ni c2", ni=NI)
    hsl = t["mk_bf"](NI * 1024 * 2, NI * 512).rearrange("p (ni tok) -> p ni tok", ni=NI)
    for chalf in range(2):
        cb0 = chalf * 1024
        if chalf == 0:
            cx.wait("sync", "gumm", cx.ctr["gumm"])
            cx.wait("sync", "hev", cx.ctr["hev"])
        else:
            cx.wait("sync", "dmm", cx.ctr["dmm"])
            cx.wait("sync", "z2st", cx.ctr["z2st"])
        cx.step("sync", lambda e, b=cb0: e.dma_start(
            out=wdh[:, :, :],
            in_=t["wdT_d"].rearrange("(ni p) c2 -> p ni c2", p=128)[:, :, b:b + 1024]),
            [("wdld", 16)])
        for ttg in range(ntt):
            a = ttg * 512
            w = min(512, n - a)
            cx.wait("sync", "dmm", cx.ctr["dmm"])
            cx.step("sync", lambda e, aa=a, ww=w: e.dma_start(
                out=hsl[:, :, 0:ww],
                in_=t["h_d"].rearrange("(ni p) tok -> p ni tok", p=128)[:, :, aa:aa + ww]),
                [("hld", 16)])
            cx.wait("tensor", "wdld", cx.ctr["wdld"])
            cx.wait("tensor", "hld")
            for tk in range(cdiv(w, 128)):
                wk = min(128, w - tk * 128)
                par2 = (cx.ctr["dmm"] // 2) % 2
                zq = 0
                cx.wait("sync", "oev", max(0, cx.ctr["oev"] - 1))
                cx.step("sync", lambda e, aa=a + tk * 128, wkk=wk, b=cb0: e.dma_start(
                    out=t["sb_res"][0:wkk, :], in_=t["x1tm_d"][aa:aa + wkk, b:b + 1024]),
                    [("hld", 16)])
                for ct in range(2):
                    bank = t["ps"][4 + 2 * par2 + ct]
                    cx.wait("tensor", "oev", max(0, cx.ctr["oev"] - 2))
                    cx.wait("tensor", "hld", cx.ctr["hld"])
                    for ic in range(NI):
                        cx.step("tensor", lambda e, i2=ic, k=tk, wkk=wk, c2=ct, bk=bank: e.matmul(
                            bk[0:wkk, 0:512], hsl[:, i2, k * 128:k * 128 + wkk],
                            wdh[:, i2, c2 * 512:(c2 + 1) * 512],
                            start=(i2 == 0), stop=(i2 == NI - 1)),
                            [("dmm", 1)] if ic == NI - 1 else ())
                cx.wait("vector", "dmm")
                if chalf == 0:
                    cx.wait("vector", "z2st", cx.ctr["z2st"])
                    for ct in range(2):
                        cx.step("vector", lambda e, wkk=wk, c2=ct, q=zq, p2=par2: e.tensor_tensor(
                            out=t["sb_z2f"][q][0:wkk, c2 * 512:(c2 + 1) * 512],
                            in0=t["ps"][4 + 2 * p2 + c2][0:wkk, 0:512],
                            in1=t["sb_res"][0:wkk, c2 * 512:(c2 + 1) * 512],
                            op=ALU.add), [("oev", 1)] if ct == 1 else ())
                    cx.wait("sync", "oev")
                    cx.step("sync", lambda e, aa=a + tk * 128, wkk=wk, q=zq: e.dma_start(
                        out=t["z2_d"][aa:aa + wkk, :], in_=t["sb_z2f"][q][0:wkk, 0:1024]),
                        [("z2st", 16)])
                else:
                    cx.wait("sync", "ofin", cx.ctr["ofin"])
                    cx.step("sync", lambda e, aa=a + tk * 128, wkk=wk, q=zq: e.dma_start(
                        out=t["sb_z2f"][q][0:wkk, 0:1024], in_=t["z2_d"][aa:aa + wkk, :]),
                        [("z2ld", 16)])
                    for ct in range(2):
                        cx.step("vector", lambda e, wkk=wk, c2=ct, q=zq, p2=par2: e.tensor_tensor(
                            out=t["sb_z2f"][q][0:wkk, 1024 + c2 * 512:1024 + (c2 + 1) * 512],
                            in0=t["ps"][4 + 2 * p2 + c2][0:wkk, 0:512],
                            in1=t["sb_res"][0:wkk, c2 * 512:(c2 + 1) * 512],
                            op=ALU.add), [("oev", 1)] if ct == 1 else ())
                    cx.wait("vector", "z2ld")
                    cx.step("vector", lambda e, wkk=wk, q=zq: e.reduce_sum(
                        t["sb_st1"][0:wkk, 0:1], t["sb_z2f"][q][0:wkk, :], axis=AX.X))
                    cx.drain("vector")
                    cx.step("vector", lambda e, wkk=wk: e.tensor_scalar(
                        out=t["sb_st1"][0:wkk, 0:1], in0=t["sb_st1"][0:wkk, 0:1],
                        scalar1=1.0 / C, scalar2=None, op0=ALU.mult))
                    cx.drain("vector")
                    cx.step("vector", lambda e: e.nop(), [("st1", 1)])
                    cx.wait("scalar", "ost", max(0, cx.ctr["ost"] - 16))
                    cx.wait("scalar", "st1", cx.ctr["st1"])
                    cx.step("scalar", lambda e, wkk=wk, q=zq: e.activation(
                        out=t["sb_of"][0][0:wkk, :], in_=t["sb_z2f"][q][0:wkk, 0:1024],
                        func=AF.Square, accum_out=t["sb_st2"][0:wkk, 0:1]))
                    cx.step("scalar", lambda e, wkk=wk, q=zq: e.activation(
                        out=t["sb_of"][0][0:wkk, :], in_=t["sb_z2f"][q][0:wkk, 1024:2048],
                        func=AF.Square, accum_out=t["sb_st3"][0:wkk, 0:1]), [("sq2", 1)])
                    cx.wait("vector", "sq2")
                    cx.step("vector", lambda e, wkk=wk: e.tensor_tensor(
                        out=t["sb_st2"][0:wkk, 0:1], in0=t["sb_st2"][0:wkk, 0:1],
                        in1=t["sb_st3"][0:wkk, 0:1], op=ALU.add))
                    cx.drain("vector")
                    cx.step("vector", lambda e, wkk=wk: e.tensor_scalar(
                        out=t["sb_st2"][0:wkk, 0:1], in0=t["sb_st2"][0:wkk, 0:1],
                        scalar1=1.0 / C, scalar2=None, op0=ALU.mult))
                    cx.drain("vector")
                    cx.step("vector", lambda e, wkk=wk: e.tensor_tensor(
                        out=t["sb_st3"][0:wkk, 0:1], in0=t["sb_st1"][0:wkk, 0:1],
                        in1=t["sb_st1"][0:wkk, 0:1], op=ALU.mult))
                    cx.drain("vector")
                    cx.step("vector", lambda e, wkk=wk: e.tensor_tensor(
                        out=t["sb_st2"][0:wkk, 0:1], in0=t["sb_st2"][0:wkk, 0:1],
                        in1=t["sb_st3"][0:wkk, 0:1], op=ALU.subtract))
                    cx.drain("vector")
                    cx.step("vector", lambda e: e.nop(), [("st2", 1)])
                    cx.wait("scalar", "st2")
                    cx.step("scalar", lambda e, wkk=wk: e.activation(
                        out=t["sb_st3"][0:wkk, 0:1], in_=t["sb_st2"][0:wkk, 0:1],
                        func=AF.Ln, bias=t["sb_eps"][0:wkk, :], scale=1.0))
                    cx.drain("scalar")
                    cx.step("scalar", lambda e, wkk=wk: e.activation(
                        out=t["sb_st4"][0:wkk, 0:1], in_=t["sb_st3"][0:wkk, 0:1],
                        func=AF.Exp, scale=-0.5, bias=t["sb_zero"][0:wkk, :]))
                    cx.drain("scalar")
                    cx.step("scalar", lambda e: e.nop(), [("ln2", 2)])
                    cx.wait("vector", "ln2")
                    cx.wait("sync", "ln2", cx.ctr["ln2"])
                    for si, stn in enumerate(("sb_st1", "sb_st2", "sb_st3", "sb_st4")):
                        cx.step("sync", lambda e, aa=a + tk * 128, wkk=wk, s2=si, sn=stn: e.dma_start(
                            out=t["st_d"][s2:s2 + 1, aa:aa + wkk].squeeze(),
                            in_=t[sn][0:wkk, 0:1].squeeze()),
                            [("ost", 16)])
                    for oh in range(2):
                        ofv = t["sb_of"][oh]
                        cx.wait("vector", "ost", max(0, cx.ctr["ost"] - 32))
                        if DBG_LN2_RAW:
                            cx.step("vector", lambda e, wkk=wk, q=zq, o2=oh, v=ofv: e.tensor_scalar(
                                out=v[0:wkk, :],
                                in0=t["sb_z2f"][q][0:wkk, o2 * 1024:(o2 + 1) * 1024],
                                scalar1=1.0, scalar2=None,
                                op0=ALU.mult), [("ofin", 1)])
                        else:
                            cx.step("vector", lambda e, wkk=wk, q=zq, o2=oh, v=ofv: e.tensor_scalar(
                                out=v[0:wkk, :],
                                in0=t["sb_z2f"][q][0:wkk, o2 * 1024:(o2 + 1) * 1024],
                                scalar1=t["sb_st1"][0:wkk, 0:1], scalar2=t["sb_st4"][0:wkk, 0:1],
                                op0=ALU.subtract, op1=ALU.mult), [("ofin", 1)])
                        cx.wait("sync", "ofin")
                        cx.step("sync", lambda e, aa=a + tk * 128, wkk=wk, o2=oh, v=ofv: e.dma_start(
                            out=t["out"][aa:aa + wkk, o2 * 1024:(o2 + 1) * 1024],
                            in_=v[0:wkk, :]), [("ost", 16)])


PHASES = 99
DUMPS = ()
DBG_NO_RESID = False
DBG_NO_YSC = False
DBG_LN2_RAW = False
DBG_DVE_SQUARE = False
DUMPABLE = {
    "wg_d": ([INTER, C], BF16), "wu_d": ([INTER, C], BF16), "wd_d": ([C, INTER], BF16),
    "xT_d": ([C, T], BF16), "kT_d": ([C, T], BF16), "qT_d": ([C, NMAX], BF16),
    "v_d": ([T, C], BF16), "y_d": ([C, NMAX], BF16), "sums_d": ([H, NMAX], F32),
    "wgT_d": ([C, INTER], BF16), "wdT_d": ([INTER, C], BF16),
    "x1T_d": ([C, NMAX], BF16), "x1tm_d": ([NMAX, C], F32), "h_d": ([INTER, NMAX], BF16),
    "z2_d": ([NMAX, 1024], F32), "st_d": ([4, NMAX], F32),
}


def emit_arm(cx, c):
    beta, j = c // 4, c % 4
    T0, T1 = BOUNDS[j], BOUNDS[j + 1]
    P = T1
    if PHASES >= 1:
        emit_gather(cx, 0)
    if PHASES >= 2:
        emit_xprep(cx, beta, P)
    if PHASES >= 3:
        emit_qkv(cx, beta, T0, T1, P)
    if PHASES >= 1:
        emit_gather(cx, 1)
    if PHASES >= 4:
        emit_attention(cx, beta, T0, T1, P)
    if PHASES >= 5:
        emit_wtrans(cx)
    if PHASES >= 6:
        emit_proj_ln1(cx, beta, T0, T1)
    if PHASES >= 7:
        emit_ffn_ln2(cx, beta, T0, T1)
    for s in ("gst", "xtst", "kst", "vst", "yst", "wtst", "x1bst", "x1st", "ost"):
        cx.wait("sync", s, cx.ctr[s])
    for dn in DUMPS:
        cx.step("sync", lambda e, d=dn: e.dma_start(
            out=cx.t["dump_" + d], in_=cx.t[d]), [("tagst", 16)])
        cx.wait("sync", "tagst")
    cx.step("sync", lambda e, cc2=c: e.dma_start(
        out=cx.t["tag"], in_=cx.t["armvals"][0:1, cc2:cc2 + 1]), [("tagst", 16)])
    cx.wait("sync", "tagst")


SEM_NAMES = [
    "ld0", "pe0", "ve0",
    "gld", "gdone", "gst", "wtld", "wtst",
    "xc", "xtld", "xtst",
    "wsl", "xsl", "kmm", "kev", "kst", "vmm", "vev", "vst",
    "kvld", "qld", "smm", "pexp", "pmsk", "pfree", "ymm", "summ",
    "yev", "smev", "yst",
    "ysl", "rec", "bcm", "ysc", "yscal", "wpld", "pmm", "xblk", "xtr", "xtoe",
    "zf", "zev", "zsq", "stm", "mu", "var", "lnv", "bc2", "bc2e", "x1f", "x1b",
    "x1tr", "x1tmev", "x1st", "x1bst",
    "x1Tld", "wqld", "gumm", "sil", "hev", "hst",
    "wdld", "hld", "dmm", "oev", "z2st", "z2ld", "st1", "sq2", "sq2r", "st2",
    "ln2", "ofin", "ost", "tagst",
]

BFN = 67072
OFF = {
    "idxb0": 0, "idxb1": 5504, "gacc0": 11008, "gacc1": 16512,
    "gtmp0": 22016, "gtmp1": 27520,
    "win": 33024,
    "hev0": 66048, "hev1": 66560,
}


def build_program():
    nc = bass.Bass("TRN2", target_bir_lowering=False)
    t = {}
    t["x"] = nc.dram_tensor("x", [B, T, C], F32, kind="ExternalInput").ap()
    t["wattn"] = nc.dram_tensor("wattn", [C, 3 * C], F32, kind="ExternalInput").ap()
    t["battn"] = nc.dram_tensor("battn", [1, 3 * C], F32, kind="ExternalInput").ap()
    t["wproj"] = nc.dram_tensor("wproj", [C, C], F32, kind="ExternalInput").ap()
    t["bproj"] = nc.dram_tensor("bproj", [1, C], F32, kind="ExternalInput").ap()
    t["lut"] = nc.dram_tensor("lut", [1, 16], F32, kind="ExternalInput").ap()
    t["gidx"] = nc.dram_tensor("gidx", [INTER, C], I32, kind="ExternalInput").ap()
    t["uidx"] = nc.dram_tensor("uidx", [INTER, C], I32, kind="ExternalInput").ap()
    t["didx"] = nc.dram_tensor("didx", [C, INTER], I32, kind="ExternalInput").ap()
    t["ones_f"] = nc.dram_tensor("ones_f", [1, 128], F32, kind="ExternalInput").ap()
    t["ones_w"] = nc.dram_tensor("ones_w", [1, 512], BF16, kind="ExternalInput").ap()
    t["onescol"] = nc.dram_tensor("onescol", [128, 1], BF16, kind="ExternalInput").ap()
    t["identf"] = nc.dram_tensor("identf", [128, 128], F32, kind="ExternalInput").ap()
    t["dmask"] = nc.dram_tensor("dmask", [4, 128, 512], BF16, kind="ExternalInput").ap()
    t["selh"] = nc.dram_tensor("selh", [16, C], BF16, kind="ExternalInput").ap()
    t["mbias"] = nc.dram_tensor("mbias", [128, 1], F32, kind="ExternalInput").ap()
    t["epsv"] = nc.dram_tensor("epsv", [128, 1], F32, kind="ExternalInput").ap()
    t["zerov"] = nc.dram_tensor("zerov", [128, 1], F32, kind="ExternalInput").ap()
    t["armvals"] = nc.dram_tensor("armvals", [1, 8], F32, kind="ExternalInput").ap()
    t["out"] = nc.dram_tensor("out", [NMAX, C], F32, kind="ExternalOutput").ap()
    t["tag"] = nc.dram_tensor("tag", [1, 1], F32, kind="ExternalOutput").ap()
    for dn in DUMPS:
        shp, dt = DUMPABLE[dn]
        t["dump_" + dn] = nc.dram_tensor("dump_" + dn, shp, dt, kind="ExternalOutput").ap()
    for nm, shp, dt in (
        ("xbf_d", [T, C], BF16), ("xT_d", [C, T], BF16), ("kT_d", [C, T], BF16),
        ("qT_d", [C, NMAX], BF16), ("v_d", [T, C], BF16), ("y_d", [C, NMAX], BF16),
        ("sums_d", [H, NMAX], F32), ("x1T_d", [C, NMAX], BF16),
        ("x1tm_d", [NMAX, C], F32), ("wg_d", [INTER, C], BF16),
        ("wu_d", [INTER, C], BF16), ("wd_d", [C, INTER], BF16),
        ("wgT_d", [C, INTER], BF16), ("wuT_d", [C, INTER], BF16),
        ("wdT_d", [INTER, C], BF16), ("h_d", [INTER, NMAX], BF16),
        ("z2_d", [NMAX, 1024], F32), ("st_d", [4, NMAX], F32),
    ):
        t[nm] = nc.dram_tensor(nm, shp, dt).ap()

    from contextlib import ExitStack
    es = ExitStack()
    arena_bf = es.enter_context(nc.sbuf_tensor("arena_bf", [128, BFN], BF16))

    def bfv(off, nelem):
        assert off + nelem <= BFN, (off, nelem)
        return arena_bf[:, off:off + nelem]

    def mk_bf(byte_off, nelem):
        return bfv(byte_off // 2, nelem)

    t["mk_bf"] = mk_bf
    t["sb_idxb"] = [bfv(OFF["idxb0"], INTER), bfv(OFF["idxb1"], INTER)]
    t["sb_gacc"] = [bfv(OFF["gacc0"], INTER), bfv(OFF["gacc1"], INTER)]
    t["sb_gtmp"] = [bfv(OFF["gtmp0"], INTER), bfv(OFF["gtmp1"], INTER)]
    t["sb_hev"] = [bfv(OFF["hev0"], 512), bfv(OFF["hev1"], 512)]
    W0 = OFF["win"]
    t["v_wslab"] = bfv(W0, CC * 1024).rearrange("p (cc d) -> p cc d", cc=CC)
    t["v_xts"] = bfv(W0 + 16384, CC * 512).rearrange("p (cc w) -> p cc w", cc=CC)
    t["v_kev"] = [bfv(W0 + 24576 + i * 512, 512) for i in range(3)]
    t["v_vev"] = [bfv(W0 + 26112 + i * 1024, 1024) for i in range(2)]
    t["v_K"] = bfv(W0, T)
    t["v_V"] = bfv(W0 + 8192, (T // 128) * 128).rearrange("p (ck d) -> p ck d", ck=T // 128)
    t["v_q"] = [bfv(W0 + 16384 + i * 512, 512) for i in range(2)]
    t["v_PT"] = [bfv(W0 + 17408 + i * 512, 512) for i in range(4)]
    t["v_yev"] = [bfv(W0 + 19456 + i * 512, 512) for i in range(2)]
    t["v_wp"] = bfv(W0, CC * 512).rearrange("p (cc d) -> p cc d", cc=CC)
    t["v_ysl"] = bfv(W0 + 8192, H * 512).rearrange("p (hh w) -> p hh w", hh=H)
    t["v_ysc"] = bfv(W0 + 16384, H * 512).rearrange("p (hh w) -> p hh w", hh=H)
    t["v_zb"] = bfv(W0 + 16384, CC * 512).rearrange("p (cc w) -> p cc w", cc=CC)
    t["v_zsq"] = bfv(W0 + 24576, 512)
    t["v_x1bb"] = [bfv(W0 + 25088, 512), bfv(W0 + 26112, 512)]
    t["sb_rec16b"] = bfv(W0 + 25600, 512)

    FN = 10250
    arena_f = es.enter_context(nc.sbuf_tensor("arena_f", [128, FN], F32))
    fo = [0]

    def af(nelem):
        o = fo[0]
        fo[0] += nelem
        assert fo[0] <= FN, fo[0]
        return arena_f[:, o:o + nelem]

    t["sb_smev"] = [af(512), af(512)]
    t["sb_sums16"] = af(512)
    t["sb_rec16"] = af(512)
    t["sb_xblk"] = [af(128), af(128)]
    t["sb_xto"] = af(512)
    t["sb_z"] = af(512)
    _mublk = af(1024)
    t["sb_mub"] = _mublk[:, 0:512]
    t["sb_rstdb"] = _mublk[:, 512:1024]
    t["sb_x1f"] = af(512)
    t["sb_x1tm"] = [af(128), af(128)]
    t["sb_sil"] = af(512)
    t["sb_res"] = af(1024)
    t["sb_z2f"] = [af(2048)]
    t["sb_zsq2"] = af(512)
    t["sb_of"] = [af(1024), _mublk]
    t["sb_st1"] = af(1)
    t["sb_st2"] = af(1)
    t["sb_st3"] = af(1)
    t["sb_st4"] = af(1)

    def sb(name, shape, dt):
        return es.enter_context(nc.sbuf_tensor(name, shape, dt))

    t["sb_lut"] = sb("sb_lut", [1, 16], F32)[:]
    t["sb_ones_f"] = sb("sb_ones_f", [1, 128], F32)[:]
    t["sb_ones_w"] = sb("sb_ones_w", [1, 512], BF16)[:]
    t["sb_onescol"] = sb("sb_onescol", [128, 1], BF16)[:]
    t["sb_identf"] = sb("sb_identf", [128, 128], F32)[:]
    t["sb_dmask"] = sb("sb_dmask", [128, 4, 512], BF16)[:]
    t["sb_selh"] = sb("sb_selh", [16, C], BF16)[:]
    t["sb_mb"] = sb("sb_mb", [128, 1], F32)[:]
    t["sb_eps"] = sb("sb_eps", [128, 1], F32)[:]
    t["sb_zero"] = sb("sb_zero", [128, 1], F32)[:]
    t["sb_battn"] = sb("sb_battn", [1, 3 * C], BF16)[:]
    t["sb_bproj"] = sb("sb_bproj", [1, C], BF16)[:]
    t["sb_lutbc"] = sb("sb_lutbc", [128, 16], F32)[:]
    t["sb_mu"] = sb("sb_mu", [1, 512], F32)[:]
    t["sb_ex2"] = sb("sb_ex2", [1, 512], F32)[:]
    t["sb_mu2"] = sb("sb_mu2", [1, 512], F32)[:]
    t["sb_xtr"] = [sb("sb_xtr0", [128, 512], BF16)[:], sb("sb_xtr1", [128, 512], BF16)[:]]
    t["sb_wt"] = t["sb_xtr"]

    t["ps"] = [es.enter_context(nc.psum_tensor("ps%d" % i, [128, 512], F32))
               for i in range(8)]
    sems = {s: es.enter_context(nc.semaphore("s_" + s)) for s in SEM_NAMES}

    with es, nc.Block() as block:
        arm_map = _ensure_arms()

        def run_pass(E, eng):
            cx0 = Ctx(E, eng, t, sems)
            emit_prologue(cx0)
            for arm in eng.switch_core_id(n=128):
                idx = int(arm)
                if idx in arm_map:
                    emit_arm(Ctx(E, eng, t, sems, ctr=cx0.ctr), arm_map[idx])

        @block.sync
        def _(sync):
            run_pass("sync", sync)

        @block.tensor
        def _(tensor):
            run_pass("tensor", tensor)

        @block.vector
        def _(vector):
            run_pass("vector", vector)

        @block.scalar
        def _(scalar):
            run_pass("scalar", scalar)

        @block.gpsimd
        def _(gpsimd):
            run_pass("gpsimd", gpsimd)

    return nc


_NC_CACHE = {}


def _discover_arms():
    nc = bass.Bass("TRN2", target_bir_lowering=False)
    pidx = nc.dram_tensor("pidx", [1, 128], F32, kind="ExternalInput").ap()
    py = nc.dram_tensor("py", [1, 1], F32, kind="ExternalOutput").ap()
    from contextlib import ExitStack
    es = ExitStack()
    st = es.enter_context(nc.semaphore("st"))
    with es, nc.Block() as block:
        @block.sync
        def _(sync):
            for arm in sync.switch_core_id(n=128):
                k = int(arm)
                sync.dma_start(out=py, in_=pidx[0:1, k:k + 1]).then_inc(st, 16)
            sync.wait_ge(st, 16)
    ar = np.arange(128, dtype=np.float32).reshape(1, 128)
    res = run_bass_kernel_spmd(nc, [{"pidx": ar} for _ in range(8)],
                               core_ids=list(range(8)))
    arm_map = {}
    for r in range(8):
        k = int(round(float(np.asarray(res.results[r]["py"])[0, 0])))
        arm_map[k] = r
    assert len(arm_map) == 8, f"arm discovery failed: {arm_map}"
    return arm_map


_ARM_MAP = None


def _ensure_arms():
    global _ARM_MAP
    if _ARM_MAP is None:
        _ARM_MAP = _discover_arms()
    return _ARM_MAP


def kernel(**inputs):
    x = np.ascontiguousarray(np.asarray(inputs["L0_attn_x"]), dtype=np.float32)
    wattn = np.ascontiguousarray(np.asarray(inputs["L0_attn_W_attn"]), dtype=np.float32)
    battn = np.ascontiguousarray(np.asarray(inputs["L0_attn_b_attn"]), dtype=np.float32).reshape(1, -1)
    wproj = np.ascontiguousarray(np.asarray(inputs["L0_attn_W_proj"]), dtype=np.float32)
    bproj = np.ascontiguousarray(np.asarray(inputs["L0_attn_b_proj"]), dtype=np.float32).reshape(1, -1)
    lut = np.ascontiguousarray(np.asarray(inputs["L0_ffn_lut"]), dtype=np.float32).reshape(1, -1)
    gidx = np.ascontiguousarray(np.asarray(inputs["L0_ffn_gate_idx"]), dtype=np.int32)
    uidx = np.ascontiguousarray(np.asarray(inputs["L0_ffn_up_idx"]), dtype=np.int32)
    didx = np.ascontiguousarray(np.asarray(inputs["L0_ffn_down_idx"]), dtype=np.int32)

    dmask = np.zeros((4, 128, 512), np.float32)
    for d in range(4):
        for tk in range(128):
            dmask[d, tk, min(512, d * 128 + tk):] = 1.0
    selh = np.zeros((16, C), np.float32)
    for hh in range(16):
        selh[hh, hh * 128:(hh + 1) * 128] = 1.0
    im = {
        "x": x, "wattn": wattn, "battn": battn, "wproj": wproj, "bproj": bproj,
        "lut": lut, "gidx": gidx, "uidx": uidx, "didx": didx,
        "ones_f": np.ones((1, 128), np.float32),
        "ones_w": np.ones((1, 512), ml_dtypes.bfloat16),
        "onescol": np.ones((128, 1), ml_dtypes.bfloat16),
        "identf": np.eye(128, dtype=np.float32),
        "dmask": dmask.astype(ml_dtypes.bfloat16),
        "selh": selh.astype(ml_dtypes.bfloat16),
        "mbias": np.full((128, 1), MB, np.float32),
        "epsv": np.full((128, 1), EPS, np.float32),
        "zerov": np.zeros((128, 1), np.float32),
        "armvals": np.arange(1, 9, dtype=np.float32).reshape(1, 8),
    }
    if "nc" not in _NC_CACHE:
        _NC_CACHE["nc"] = build_program()
    nc = _NC_CACHE["nc"]
    res = run_bass_kernel_spmd(nc, [dict(im) for _ in range(8)], core_ids=list(range(8)))

    out = np.zeros((B, T, C), np.float32)
    seen = set()
    for r in range(8):
        tagv = int(round(float(np.asarray(res.results[r]["tag"])[0, 0]))) - 1
        assert 0 <= tagv < 8 and tagv not in seen, f"bad tags: rank {r} -> {tagv} ({seen})"
        seen.add(tagv)
        beta, j = tagv // 4, tagv % 4
        T0, T1 = BOUNDS[j], BOUNDS[j + 1]
        out[beta, T0:T1, :] = np.asarray(res.results[r]["out"])[0:T1 - T0, :]
    return out


_ensure_arms()


# version 06
# speedup vs baseline: 1.0151x; 1.0151x over previous
import numpy as np
import ml_dtypes
import concourse.bass as bass
import concourse.mybir as mybir
from concourse.bass_utils import run_bass_kernel_spmd

F32 = mybir.dt.float32
BF16 = mybir.dt.bfloat16
I32 = mybir.dt.int32
AF = mybir.ActivationFunctionType
ALU = mybir.AluOpType
AX = mybir.AxisListType

B, T, C, H, HD = 2, 8192, 2048, 16, 128
CC = C // 128
INTER = 5504
NI = INTER // 128
SCALE = float(1.0 / np.sqrt(HD))
MB = -30.0
EPS = 1e-5
BOUNDS = (0, 3072, 5248, 6912, 8192)
NMAX = 3072


def cdiv(a, b):
    return (a + b - 1) // b


class Ctx:
    def __init__(self, E, eng, t, sems, ctr=None):
        self.E = E
        self.eng = eng
        self.t = t
        self.sems = sems
        self.ctr = dict(ctr) if ctr else {k: 0 for k in sems}

    def step(self, owner, emit, incs=()):
        if self.E == owner:
            ins = emit(self.eng)
            for s, n in incs:
                ins = ins.then_inc(self.sems[s], n)
        for s, n in incs:
            self.ctr[s] += n

    def wait(self, owner, sem, val=None):
        if val is None:
            val = self.ctr[sem]
        if val <= 0:
            return
        if self.E == owner:
            self.eng.wait_ge(self.sems[sem], val)

    def drain(self, owner):
        if self.E == owner:
            self.eng.drain()



def emit_prologue(cx):
    t = cx.t
    for dst, src in (
        (t["sb_lut"], t["lut"]),
        (t["sb_ones_f"], t["ones_f"]),
        (t["sb_ones_w"], t["ones_w"]),
        (t["sb_onescol"], t["onescol"]),
        (t["sb_identf"], t["identf"]),
        (t["sb_dmask"], t["dmask"].rearrange("d p w -> p d w")),
        (t["sb_selh"], t["selh"]),
        (t["sb_mb"], t["mbias"]),
        (t["sb_eps"], t["epsv"]),
        (t["sb_zero"], t["zerov"]),
    ):
        cx.step("sync", lambda e, d=dst, s=src: e.dma_start(out=d, in_=s), [("ld0", 16)])
    cx.step("gpsimd", lambda e: e.dma_start(out=t["sb_battn"], in_=t["battn"]), [("ld0", 16)])
    cx.step("gpsimd", lambda e: e.dma_start(out=t["sb_bproj"], in_=t["bproj"]), [("ld0", 16)])
    cx.wait("tensor", "ld0")
    cx.step("tensor", lambda e: e.matmul(t["ps"][6][0:128, 0:16], t["sb_ones_f"],
                                         t["sb_lut"], start=True, stop=True), [("pe0", 1)])
    cx.wait("vector", "pe0")
    cx.step("vector", lambda e: e.tensor_copy(t["sb_lutbc"], t["ps"][6][0:128, 0:16]),
            [("ve0", 1)])


def gather_jobs():
    jobs = []
    for (srcn, dstn, R, W) in (("didx", "wd_d", C, INTER),
                               ("gidx", "wg_d", INTER, C),
                               ("uidx", "wu_d", INTER, C)):
        for r0 in range(0, R, 128):
            jobs.append((srcn, dstn, r0, W))
    return jobs


def emit_gather(cx, seg):
    t = cx.t
    jobs = gather_jobs()
    jobs = jobs[:40] if seg == 0 else jobs[40:]
    cx.wait("vector", "ve0", 1)
    for (srcn, dstn, r0, W) in jobs:
        src, dst = t[srcn], t[dstn]
        jn = cx.ctr["gld"] // 16
        sl = jn % 2
        cx.wait("gpsimd", "gst", max(0, cx.ctr["gst"] - 16))
        cx.step("gpsimd", lambda e, s=src, a=r0, w=W, q=sl: e.dma_start(
            out=t["sb_idxb"][q][:, 0:w], in_=s[a:a + 128, :]), [("gld", 16)])
        cx.wait("vector", "gld")
        cx.wait("vector", "gst", max(0, cx.ctr["gst"] - 16))
        for k in range(16):
            dd = t["sb_gacc"][sl] if k == 0 else t["sb_gtmp"][sl]
            cx.step("vector", lambda e, w=W, q=sl, kk=k, d2=dd: e.tensor_scalar(
                out=d2[:, 0:w], in0=t["sb_idxb"][q][:, 0:w],
                scalar1=float(kk), scalar2=t["sb_lutbc"][:, kk:kk + 1],
                op0=ALU.is_equal, op1=ALU.mult))
            if k > 0:
                cx.step("vector", lambda e, w=W, q=sl: e.tensor_tensor(
                    out=t["sb_gacc"][q][:, 0:w], in0=t["sb_gacc"][q][:, 0:w],
                    in1=t["sb_gtmp"][q][:, 0:w], op=ALU.add),
                    [("gdone", 1)] if k == 15 else ())
        cx.wait("sync", "gdone")
        cx.step("sync", lambda e, d=dst, a=r0, w=W, q=sl: e.dma_start(
            out=d[a:a + 128, :], in_=t["sb_gacc"][q][:, 0:w]), [("gst", 16)])


def emit_wtrans(cx):
    t = cx.t
    cx.wait("scalar", "gst")
    jobs = []
    for (srcn, dstn) in (("wg_d", "wgT_d"), ("wu_d", "wuT_d")):
        for c0 in range(0, C, 128):
            for i0 in range(0, INTER, 512):
                iw = min(512, INTER - i0)
                jobs.append((srcn, dstn, i0, c0, iw, 128, c0, i0))
    for i0 in range(0, INTER, 128):
        for c0 in range(0, C, 512):
            jobs.append(("wd_d", "wdT_d", c0, i0, 512, 128, i0, c0))
    for (srcn, dstn, r0, c0, rh, cw, dr, dc) in jobs:
        src, dst = t[srcn], t[dstn]
        jn = cx.ctr["wtld"] // 16
        sl = jn % 2
        cx.wait("scalar", "wtst", max(0, cx.ctr["wtst"] - 16))
        cx.step("scalar", lambda e, s=src, a=r0, b=c0, hh=rh, w=cw, q=sl: e.dma_start(
            out=t["sb_wt"][q][0:w, 0:hh], in_=s[a:a + hh, b:b + w], transpose=True),
            [("wtld", 16)])
        cx.wait("scalar", "wtld")
        cx.step("scalar", lambda e, d=dst, a=dr, b=dc, hh=rh, w=cw, q=sl: e.dma_start(
            out=d[a:a + w, b:b + hh], in_=t["sb_wt"][q][0:w, 0:hh]), [("wtst", 16)])


def emit_xprep(cx, beta, P):
    t = cx.t
    cx.step("gpsimd", lambda e, bb=beta, pp=P: e.dma_start(
        out=t["xbf_d"][0:pp, :], in_=t["x"][bb:bb + 1, 0:pp, :].squeeze()), [("xc", 16)])
    cx.wait("sync", "xc")
    for tt in range(cdiv(P, 512)):
        w = min(512, P - tt * 512)
        for ccb in range(CC):
            sl = (tt * CC + ccb) % 2
            cx.wait("sync", "xtst", max(0, cx.ctr["xtst"] - 16))
            cx.step("sync", lambda e, a=tt * 512, b=ccb * 128, ww=w, q=sl: e.dma_start(
                out=t["sb_xtr"][q][0:128, 0:ww], in_=t["xbf_d"][a:a + ww, b:b + 128],
                transpose=True), [("xtld", 16)])
            cx.wait("sync", "xtld")
            cx.step("sync", lambda e, a=ccb * 128, b=tt * 512, ww=w, q=sl: e.dma_start(
                out=t["xT_d"][a:a + 128, b:b + ww], in_=t["sb_xtr"][q][0:128, 0:ww]),
                [("xtst", 16)])


def emit_qkv(cx, beta, T0, T1, P):
    t = cx.t
    wslab = t["v_wslab"]
    xts = t["v_xts"]
    kev = t["v_kev"]
    vev = t["v_vev"]
    for which, colbase, tok0, tok1, dstn in (("q", 0, T0, T1, "qT_d"),
                                             ("k", C, 0, P, "kT_d")):
        dstd = t[dstn]
        for ch2 in range(2):
            cb = colbase + ch2 * 1024
            cx.wait("gpsimd", "kmm", cx.ctr["kmm"])
            cx.wait("gpsimd", "vmm", cx.ctr["vmm"])
            cx.step("gpsimd", lambda e, b=cb: e.dma_start(
                out=wslab,
                in_=t["wattn"].rearrange("(cc p) d -> p cc d", p=128)[:, :, b:b + 1024]),
                [("wsl", 16)])
            cx.wait("tensor", "wsl")
            ntt = cdiv(tok1 - tok0, 512)
            for tt in range(ntt):
                a = tok0 + tt * 512
                w = min(512, tok1 - a)
                cx.wait("sync", "kmm", cx.ctr["kmm"])
                cx.step("sync", lambda e, aa=a, ww=w: e.dma_start(
                    out=xts[:, :, 0:ww],
                    in_=t["xT_d"].rearrange("(cc p) tok -> p cc tok", p=128)[:, :, aa:aa + ww]),
                    [("xsl", 16)])
                cx.wait("tensor", "xsl")
                for dc in range(8):
                    bank = t["ps"][dc % 2]
                    cx.wait("tensor", "kev", max(0, cx.ctr["kev"] - 1))
                    for c2 in range(CC):
                        cx.step("tensor", lambda e, cc2=c2, d=dc, ww=w, bk=bank: e.matmul(
                            bk[:, 0:ww], wslab[:, cc2, d * 128:(d + 1) * 128],
                            xts[:, cc2, 0:ww], start=(cc2 == 0), stop=False))
                    cx.step("tensor", lambda e, d=dc, b2=cb, ww=w, bk=bank: e.matmul(
                        bk[:, 0:ww], t["sb_battn"][0:1, b2 + d * 128:b2 + (d + 1) * 128],
                        t["sb_ones_w"][0:1, 0:ww], start=False, stop=True), [("kmm", 1)])
                    cx.wait("scalar", "kmm")
                    ev = kev[dc % 3]
                    cx.wait("scalar", "kst", max(0, cx.ctr["kst"] - 32))
                    cx.step("scalar", lambda e, ww=w, bk=bank, v=ev: e.copy(
                        out=v[:, 0:ww], in_=bk[:, 0:ww]), [("kev", 1)])
                    cx.wait("sync", "kev")
                    row = ch2 * 1024 + dc * 128
                    aa = (a - tok0) if which == "q" else a
                    cx.step("sync", lambda e, r=row, a2=aa, ww=w, v=ev, dd=dstd: e.dma_start(
                        out=dd[r:r + 128, a2:a2 + ww], in_=v[:, 0:ww]), [("kst", 16)])
    for ch2 in range(2):
        cb = 2 * C + ch2 * 1024
        cx.wait("gpsimd", "kmm", cx.ctr["kmm"])
        cx.wait("gpsimd", "vmm", cx.ctr["vmm"])
        cx.step("gpsimd", lambda e, b=cb: e.dma_start(
            out=wslab,
            in_=t["wattn"].rearrange("(cc p) d -> p cc d", p=128)[:, :, b:b + 1024]),
            [("wsl", 16)])
        cx.wait("tensor", "wsl")
        ntt = cdiv(P, 512)
        for tt in range(ntt):
            a = tt * 512
            w = min(512, P - a)
            cx.wait("sync", "vmm", cx.ctr["vmm"])
            cx.step("sync", lambda e, aa=a, ww=w: e.dma_start(
                out=xts[:, :, 0:ww],
                in_=t["xT_d"].rearrange("(cc p) tok -> p cc tok", p=128)[:, :, aa:aa + ww]),
                [("xsl", 16)])
            cx.wait("tensor", "xsl")
            for tk in range(cdiv(w, 128)):
                wk = min(128, w - tk * 128)
                for vc in range(2):
                    bank = t["ps"][4 + vc]
                    cx.wait("tensor", "vev", cx.ctr["vev"])
                    for c2 in range(CC):
                        cx.step("tensor", lambda e, cc2=c2, k=tk, wkk=wk, v=vc, bk=bank: e.matmul(
                            bk[0:wkk, 0:512], xts[:, cc2, k * 128:k * 128 + wkk],
                            wslab[:, cc2, v * 512:(v + 1) * 512],
                            start=(cc2 == 0), stop=False))
                    cx.step("tensor", lambda e, v=vc, b2=cb, wkk=wk, bk=bank: e.matmul(
                        bk[0:wkk, 0:512], t["sb_ones_w"][0:1, 0:wkk],
                        t["sb_battn"][0:1, b2 + v * 512:b2 + (v + 1) * 512],
                        start=False, stop=True), [("vmm", 1)])
                cx.wait("scalar", "vmm")
                vv = vev[tk % 2]
                cx.wait("scalar", "vst", max(0, cx.ctr["vst"] - 16))
                for vc in range(2):
                    cx.step("scalar", lambda e, v=vc, wkk=wk, v2=vv: e.copy(
                        out=v2[0:wkk, v * 512:(v + 1) * 512],
                        in_=t["ps"][4 + v][0:wkk, 0:512]), [("vev", 1)])
                cx.wait("sync", "vev")
                cx.step("sync", lambda e, aa=a + tk * 128, wkk=wk, v2=vv, b2=ch2: e.dma_start(
                    out=t["v_d"][aa:aa + wkk, b2 * 1024:(b2 + 1) * 1024],
                    in_=v2[0:wkk, :]), [("vst", 16)])


def emit_attention(cx, beta, T0, T1, P):
    t = cx.t
    n = T1 - T0
    nqg = cdiv(n, 512)
    Kh = t["v_K"]
    Vh = t["v_V"]
    qtl = t["v_q"]
    PT = t["v_PT"]
    yev = t["v_yev"]
    cx.wait("tensor", "kev", cx.ctr["kev"])
    cx.wait("tensor", "vev", cx.ctr["vev"])
    cx.wait("sync", "kmm", cx.ctr["kmm"])
    cx.wait("sync", "vmm", cx.ctr["vmm"])
    cx.wait("sync", "kev", cx.ctr["kev"])
    cx.wait("sync", "vev", cx.ctr["vev"])
    for h in range(H):
        cx.wait("sync", "summ", cx.ctr["summ"])
        cx.step("sync", lambda e, hh=h, pp=P: e.dma_start(
            out=Kh[:, 0:pp], in_=t["kT_d"][hh * 128:(hh + 1) * 128, 0:pp]),
            [("kvld", 16)])
        ntka = cdiv(P, 128)
        cx.step("sync", lambda e, hh=h, nt=ntka: e.dma_start(
            out=Vh[:, 0:nt, :],
            in_=t["v_d"].rearrange("(ck p) d -> p ck d", p=128)[:, 0:nt, hh * 128:(hh + 1) * 128]),
            [("kvld", 16)])
        cx.wait("tensor", "kvld")
        for qg in range(nqg):
            q0 = qg * 512
            w = min(512, n - q0)
            qs = T0 + q0
            nfull = qs // 128
            ndiag = cdiv(w, 128)
            nch = nfull + ndiag
            cx.wait("sync", "summ", max(0, cx.ctr["summ"] - 1))
            cx.step("sync", lambda e, hh=h, a=q0, ww=w, q=qg % 2: e.dma_start(
                out=qtl[q][:, 0:ww], in_=t["qT_d"][hh * 128:(hh + 1) * 128, a:a + ww]),
                [("qld", 16)])
            cx.wait("tensor", "qld")
            ybank = t["ps"][2 + (qg % 2)]
            smbank = t["ps"][4 + (qg % 2)]
            cx.wait("tensor", "yev", max(0, cx.ctr["yev"] - 1))
            cx.wait("tensor", "smev", max(0, cx.ctr["smev"] - 1))
            for ch in range(nch):
                diag = ch >= nfull
                kw = 128 if not diag else min(128, qs + w - ch * 128)
                sb = t["ps"][(0, 1, 6)[ch % 3]]
                pt = PT[ch % 4]
                cx.wait("tensor", "pexp", max(0, cx.ctr["pexp"] - 2))
                cx.step("tensor", lambda e, a=ch * 128, kk=kw, ww=w, g=qg % 2, bk=sb: e.matmul(
                    bk[0:kk, 0:ww], Kh[:, a:a + kk], qtl[g][:, 0:ww],
                    start=True, stop=True), [("smm", 1)])
                cx.wait("scalar", "smm")
                cx.wait("scalar", "pfree", max(0, cx.ctr["pfree"] - 2))
                cx.step("scalar", lambda e, kk=kw, ww=w, bk=sb, p=pt: e.activation(
                    out=p[0:kk, 0:ww], in_=bk[0:kk, 0:ww], func=AF.Exp,
                    bias=t["sb_mb"], scale=SCALE), [("pexp", 1)])
                if diag:
                    d = ch - nfull
                    cx.wait("gpsimd", "pexp")
                    cx.step("gpsimd", lambda e, kk=kw, ww=w, p=pt, dd=d: e.tensor_tensor(
                        out=p[0:kk, 0:ww], in0=p[0:kk, 0:ww],
                        in1=t["sb_dmask"][0:kk, dd, 0:ww], op=ALU.mult), [("pmsk", 1)])
                    cx.wait("tensor", "pmsk")
                else:
                    cx.wait("tensor", "pexp")
                cx.step("tensor", lambda e, a=ch, kk=kw, ww=w, bk=ybank, p=pt: e.matmul(
                    bk[:, 0:ww], Vh[0:kk, a, :], p[0:kk, 0:ww],
                    start=(ch == 0), stop=(ch == nch - 1)),
                    [("ymm", 1)] if ch == nch - 1 else ())
                cx.step("tensor", lambda e, kk=kw, ww=w, bk=smbank, p=pt: e.matmul(
                    bk[0:1, 0:ww], t["sb_onescol"][0:kk, 0:1], p[0:kk, 0:ww],
                    start=(ch == 0), stop=(ch == nch - 1)), [("pfree", 1)])
                if ch == nch - 1:
                    cx.step("tensor", lambda e: e.nop(), [("summ", 1)])
            cx.wait("scalar", "ymm")
            cx.wait("scalar", "yst", max(0, cx.ctr["yst"] - 32))
            yv = yev[qg % 2]
            sv = t["sb_smev"][qg % 2]
            cx.step("scalar", lambda e, ww=w, bk=ybank, v=yv: e.copy(
                out=v[:, 0:ww], in_=bk[:, 0:ww]), [("yev", 1)])
            cx.step("scalar", lambda e, ww=w, bk=smbank, v=sv: e.copy(
                out=v[0:1, 0:ww], in_=bk[0:1, 0:ww]), [("smev", 1)])
            cx.wait("sync", "yev")
            cx.step("sync", lambda e, hh=h, a=q0, ww=w, v=yv: e.dma_start(
                out=t["y_d"][hh * 128:(hh + 1) * 128, a:a + ww], in_=v[:, 0:ww]),
                [("yst", 16)])
            cx.wait("sync", "smev")
            cx.step("sync", lambda e, hh=h, a=q0, ww=w, v=sv: e.dma_start(
                out=t["sums_d"][hh:hh + 1, a:a + ww], in_=v[0:1, 0:ww]), [("yst", 16)])


def emit_proj_ln1(cx, beta, T0, T1):
    t = cx.t
    n = T1 - T0
    wp = t["v_wp"]
    ysl = t["v_ysl"]
    ysc = t["v_ysc"]
    zb = t["v_zb"]
    zsqb = t["v_zsq"]
    x1bb = t["v_x1bb"]
    cx.wait("sync", "summ", cx.ctr["summ"])
    cx.wait("sync", "yev", cx.ctr["yev"])
    cx.wait("tensor", "pexp", cx.ctr["pexp"])
    cx.wait("tensor", "pmsk", cx.ctr["pmsk"])
    cx.wait("tensor", "yev", cx.ctr["yev"])
    cx.wait("tensor", "smev", cx.ctr["smev"])
    ntt = cdiv(n, 512)
    for tt in range(ntt):
        a = tt * 512
        w = min(512, n - a)
        cx.wait("sync", "x1b", cx.ctr["x1b"])
        cx.step("sync", lambda e, aa=a, ww=w: e.dma_start(
            out=ysl[:, :, 0:ww],
            in_=t["y_d"].rearrange("(hh p) tok -> p hh tok", p=128)[:, :, aa:aa + ww]),
            [("ysl", 16)])
        cx.step("sync", lambda e, aa=a, ww=w: e.dma_start(
            out=t["sb_sums16"][0:H, 0:ww], in_=t["sums_d"][:, aa:aa + ww]), [("ysl", 16)])
        cx.wait("vector", "ysl")
        cx.step("vector", lambda e, ww=w: e.reciprocal(
            t["sb_rec16"][0:H, 0:ww], t["sb_sums16"][0:H, 0:ww]))
        cx.drain("vector")
        cx.step("vector", lambda e, ww=w: e.tensor_copy(
            t["sb_rec16b"][0:H, 0:ww], t["sb_rec16"][0:H, 0:ww]), [("rec", 1)])
        for hh in range(H):
            cx.wait("tensor", "rec", cx.ctr["rec"])
            cx.wait("tensor", "ysc", cx.ctr["ysc"])
            cx.step("tensor", lambda e, h2=hh, ww=w: e.matmul(
                t["ps"][6][:, 0:ww], t["sb_selh"][0:H, h2 * 128:(h2 + 1) * 128],
                t["sb_rec16b"][0:H, 0:ww], start=True, stop=True), [("bcm", 1)])
            cx.wait("scalar", "bcm")
            cx.step("scalar", lambda e, h2=hh, ww=w: e.copy(
                out=ysc[:, h2, 0:ww], in_=t["ps"][6][:, 0:ww]), [("ysc", 1)])
        cx.wait("vector", "ysc", cx.ctr["ysc"])
        if not DBG_NO_YSC:
            cx.step("vector", lambda e, ww=w: e.tensor_tensor(
                out=ysl[:, :, 0:ww], in0=ysl[:, :, 0:ww], in1=ysc[:, :, 0:ww],
                op=ALU.mult), [("yscal", 1)])
        else:
            cx.step("vector", lambda e: e.nop(), [("yscal", 1)])
        cx.wait("tensor", "yscal", cx.ctr["yscal"])
        for cq in range(4):
            cx.wait("gpsimd", "pmm", cx.ctr["pmm"])
            cx.step("gpsimd", lambda e, q2=cq: e.dma_start(
                out=wp,
                in_=t["wproj"].rearrange("(cc p) d -> p cc d", p=128)[:, :, q2 * 512:(q2 + 1) * 512]),
                [("wpld", 16)])
            cx.wait("tensor", "wpld")
            for c4 in range(4):
                co = cq * 4 + c4
                bank = t["ps"][co % 2]
                for hh in range(H):
                    cx.step("tensor", lambda e, h2=hh, c2=c4, ww=w, bk=bank: e.matmul(
                        bk[:, 0:ww], wp[:, h2, c2 * 128:(c2 + 1) * 128],
                        ysl[:, h2, 0:ww], start=(h2 == 0), stop=False))
                cx.step("tensor", lambda e, c2=co, ww=w, bk=bank: e.matmul(
                    bk[:, 0:ww], t["sb_bproj"][0:1, c2 * 128:(c2 + 1) * 128],
                    t["sb_ones_w"][0:1, 0:ww], start=False, stop=True), [("pmm", 1)])
                for bi in range(cdiv(w, 128)):
                    bw = min(128, w - bi * 128)
                    slot = (cx.ctr["xblk"] // 16) % 2
                    cx.wait("sync", "xtr", cx.ctr["xtr"])
                    cx.step("sync", lambda e, aa=T0 + a + bi * 128, c2=co, bb=bw, q=slot: e.dma_start(
                        out=t["sb_xblk"][q][0:bb, :],
                        in_=t["x"][beta:beta + 1, aa:aa + bb, c2 * 128:(c2 + 1) * 128].squeeze()),
                        [("xblk", 16)])
                    cx.wait("tensor", "xblk")
                    cx.wait("tensor", "xtoe", cx.ctr["xtoe"])
                    cx.wait("tensor", "x1tmev", cx.ctr["x1tmev"])
                    cx.step("tensor", lambda e, bb=bw, q=slot: e.transpose(
                        t["ps"][3][:, 0:bb], t["sb_xblk"][q][0:bb, :], t["sb_identf"]),
                        [("xtr", 1)])
                    cx.wait("vector", "xtr")
                    cx.step("vector", lambda e, bb=bw, b2=bi: e.tensor_copy(
                        t["sb_xto"][:, b2 * 128:b2 * 128 + bb], t["ps"][3][:, 0:bb]),
                        [("xtoe", 1)])
                cx.wait("vector", "pmm")
                cx.wait("vector", "zsq", cx.ctr["zsq"])
                if not DBG_NO_RESID:
                    cx.step("vector", lambda e, ww=w, bk=bank: e.tensor_tensor(
                        out=t["sb_z"][:, 0:ww], in0=bk[:, 0:ww], in1=t["sb_xto"][:, 0:ww],
                        op=ALU.add), [("zf", 1)])
                else:
                    cx.step("vector", lambda e, ww=w, bk=bank: e.tensor_scalar(
                        out=t["sb_z"][:, 0:ww], in0=bk[:, 0:ww], scalar1=1.0,
                        scalar2=None, op0=ALU.mult), [("zf", 1)])
                cx.step("vector", lambda e, c2=co, ww=w: e.tensor_copy(
                    zb[:, c2, 0:ww], t["sb_z"][:, 0:ww]), [("zev", 1)])
                cx.wait("scalar", "zf")
                cx.step("scalar", lambda e, ww=w: e.activation(
                    out=zsqb[:, 0:ww], in_=t["sb_z"][:, 0:ww], func=AF.Square),
                    [("zsq", 1)])
                cx.wait("tensor", "zsq")
                cx.wait("tensor", "zev", cx.ctr["zev"])
                cx.step("tensor", lambda e, c2=co, ww=w: e.matmul(
                    t["ps"][4][0:1, 0:ww], t["sb_onescol"], zb[:, c2, 0:ww],
                    start=(c2 == 0), stop=(c2 == CC - 1)),
                    [("stm", 1)] if co == CC - 1 else ())
                cx.step("tensor", lambda e, c2=co, ww=w: e.matmul(
                    t["ps"][5][0:1, 0:ww], t["sb_onescol"], zsqb[:, 0:ww],
                    start=(c2 == 0), stop=(c2 == CC - 1)),
                    [("stm", 1)] if co == CC - 1 else ())
        cx.wait("scalar", "stm")
        cx.step("scalar", lambda e, ww=w: e.activation(
            out=t["sb_mu"][0:1, 0:ww], in_=t["ps"][4][0:1, 0:ww], func=AF.Copy,
            bias=0.0, scale=1.0 / C), [("mu", 1)])
        cx.step("scalar", lambda e, ww=w: e.activation(
            out=t["sb_ex2"][0:1, 0:ww], in_=t["ps"][5][0:1, 0:ww], func=AF.Copy,
            bias=0.0, scale=1.0 / C), [("mu", 1)])
        cx.step("scalar", lambda e, ww=w: e.activation(
            out=t["sb_mu2"][0:1, 0:ww], in_=t["sb_mu"][0:1, 0:ww], func=AF.Square),
            [("mu", 1)])
        cx.wait("vector", "mu")
        cx.step("vector", lambda e, ww=w: e.tensor_tensor(
            out=t["sb_ex2"][0:1, 0:ww], in0=t["sb_ex2"][0:1, 0:ww],
            in1=t["sb_mu2"][0:1, 0:ww], op=ALU.subtract), [("var", 1)])
        cx.wait("scalar", "var")
        cx.step("scalar", lambda e, ww=w: e.activation(
            out=t["sb_mu2"][0:1, 0:ww], in_=t["sb_ex2"][0:1, 0:ww], func=AF.Ln,
            bias=t["sb_eps"][0:1, :], scale=1.0), [("lnv", 1)])
        cx.step("scalar", lambda e, ww=w: e.activation(
            out=t["sb_ex2"][0:1, 0:ww], in_=t["sb_mu2"][0:1, 0:ww], func=AF.Exp,
            scale=-0.5, bias=t["sb_zero"][0:1, :]), [("lnv", 1)])
        cx.wait("tensor", "lnv")
        cx.step("tensor", lambda e, ww=w: e.matmul(
            t["ps"][4][:, 0:ww], t["sb_ones_f"], t["sb_mu"][0:1, 0:ww],
            start=True, stop=True), [("bc2", 1)])
        cx.step("tensor", lambda e, ww=w: e.matmul(
            t["ps"][5][:, 0:ww], t["sb_ones_f"], t["sb_ex2"][0:1, 0:ww],
            start=True, stop=True), [("bc2", 1)])
        cx.wait("scalar", "bc2")
        cx.step("scalar", lambda e, ww=w: e.copy(
            out=t["sb_mub"][:, 0:ww], in_=t["ps"][4][:, 0:ww]), [("bc2e", 1)])
        cx.step("scalar", lambda e, ww=w: e.copy(
            out=t["sb_rstdb"][:, 0:ww], in_=t["ps"][5][:, 0:ww]), [("bc2e", 1)])
        cx.wait("vector", "bc2e")
        for co in range(CC):
            cx.wait("vector", "x1b", cx.ctr["x1b"])
            cx.wait("vector", "x1tr", cx.ctr["x1tr"])
            cx.step("vector", lambda e, c2=co, ww=w: e.tensor_tensor(
                out=t["sb_x1f"][:, 0:ww], in0=zb[:, c2, 0:ww],
                in1=t["sb_mub"][:, 0:ww], op=ALU.subtract))
            cx.step("vector", lambda e, ww=w: e.tensor_tensor(
                out=t["sb_x1f"][:, 0:ww], in0=t["sb_x1f"][:, 0:ww],
                in1=t["sb_rstdb"][:, 0:ww], op=ALU.mult), [("x1f", 1)])
            cx.wait("scalar", "x1f")
            cx.wait("scalar", "x1bst", max(0, cx.ctr["x1bst"] - 16))
            xbslot = x1bb[co % 2]
            cx.step("scalar", lambda e, ww=w, v=xbslot: e.copy(
                out=v[:, 0:ww], in_=t["sb_x1f"][:, 0:ww]), [("x1b", 1)])
            cx.wait("sync", "x1b")
            cx.step("sync", lambda e, aa=a, c2=co, ww=w, v=xbslot: e.dma_start(
                out=t["x1T_d"][c2 * 128:(c2 + 1) * 128, aa:aa + ww],
                in_=v[:, 0:ww]), [("x1bst", 16)])
            for bi in range(cdiv(w, 128)):
                bw = min(128, w - bi * 128)
                tms = cx.ctr["x1tmev"] % 2
                cx.wait("tensor", "x1f", cx.ctr["x1f"])
                cx.wait("tensor", "x1tmev", cx.ctr["x1tmev"])
                cx.step("tensor", lambda e, b2=bi, bb=bw: e.transpose(
                    t["ps"][3][0:bb, 0:128], t["sb_x1f"][:, b2 * 128:b2 * 128 + bb],
                    t["sb_identf"]), [("x1tr", 1)])
                cx.wait("vector", "x1tr")
                cx.wait("vector", "x1st", max(0, cx.ctr["x1st"] - 16))
                cx.step("vector", lambda e, bb=bw, q=tms: e.tensor_copy(
                    t["sb_x1tm"][q][0:bb, :], t["ps"][3][0:bb, 0:128]), [("x1tmev", 1)])
                cx.wait("sync", "x1tmev")
                cx.step("sync", lambda e, aa=a + bi * 128, c2=co, bb=bw, q=tms: e.dma_start(
                    out=t["x1tm_d"][aa:aa + bb, c2 * 128:(c2 + 1) * 128],
                    in_=t["sb_x1tm"][q][0:bb, :]), [("x1st", 16)])


def emit_ffn_ln2(cx, beta, T0, T1):
    t = cx.t
    n = T1 - T0
    ntt = cdiv(n, 512)
    x1T = t["mk_bf"](0, CC * n).rearrange("p (cc tok) -> p cc tok", cc=CC)
    SW = 512
    wg_sl = t["mk_bf"](49408 * 2, CC * SW).rearrange("p (cc i) -> p cc i", cc=CC)
    wu_sl = t["mk_bf"]((49408 + CC * SW) * 2, CC * SW).rearrange("p (cc i) -> p cc i", cc=CC)
    cx.wait("sync", "x1bst", cx.ctr["x1bst"])
    cx.wait("sync", "x1st", cx.ctr["x1st"])
    cx.wait("sync", "wtst", cx.ctr["wtst"])
    cx.wait("tensor", "zf", cx.ctr["zf"])
    cx.wait("tensor", "x1tmev", cx.ctr["x1tmev"])
    cx.wait("tensor", "bc2e", cx.ctr["bc2e"])
    cx.step("sync", lambda e: e.dma_start(
        out=x1T[:, :, 0:n],
        in_=t["x1T_d"].rearrange("(cc p) tok -> p cc tok", p=128)[:, :, 0:n]),
        [("x1Tld", 16)])
    cx.wait("tensor", "x1Tld")
    nq = cdiv(INTER, SW)
    for iq in range(nq):
        i0 = iq * SW
        iw = min(SW, INTER - i0)
        cx.wait("sync", "gumm", cx.ctr["gumm"])
        for (wtn, sl2) in (("wgT_d", wg_sl), ("wuT_d", wu_sl)):
            cx.step("sync", lambda e, s=t[wtn], a=i0, ww=iw, d=sl2: e.dma_start(
                out=d[:, :, 0:ww],
                in_=s.rearrange("(cc p) i -> p cc i", p=128)[:, :, a:a + ww]),
                [("wqld", 16)])
        cx.wait("tensor", "wqld")
        for it in range(cdiv(iw, 128)):
            ib = min(128, iw - it * 128)
            for tt in range(ntt):
                w = min(512, n - tt * 512)
                par = cx.ctr["gumm"] % 2
                gbank = t["ps"][0 + par]
                ubank = t["ps"][2 + par]
                cx.wait("tensor", "hev", max(0, cx.ctr["hev"] - 2))
                for c2 in range(CC):
                    cx.step("tensor", lambda e, cc2=c2, i2=it, ww=w, t2=tt, ii=ib, bk=gbank: e.matmul(
                        bk[0:ii, 0:ww], wg_sl[:, cc2, i2 * 128:i2 * 128 + ii],
                        x1T[:, cc2, t2 * 512:t2 * 512 + ww],
                        start=(cc2 == 0), stop=(cc2 == CC - 1)))
                for c2 in range(CC):
                    cx.step("tensor", lambda e, cc2=c2, i2=it, ww=w, t2=tt, ii=ib, bk=ubank: e.matmul(
                        bk[0:ii, 0:ww], wu_sl[:, cc2, i2 * 128:i2 * 128 + ii],
                        x1T[:, cc2, t2 * 512:t2 * 512 + ww],
                        start=(cc2 == 0), stop=(cc2 == CC - 1)),
                        [("gumm", 1)] if c2 == CC - 1 else ())
                cx.wait("scalar", "gumm")
                cx.step("scalar", lambda e, ww=w, ii=ib, bk=gbank: e.activation(
                    out=t["sb_sil"][0:ii, 0:ww], in_=bk[0:ii, 0:ww], func=AF.Silu),
                    [("sil", 1)])
                cx.wait("vector", "sil")
                cx.wait("vector", "hst", max(0, cx.ctr["hst"] - 32))
                hv = t["sb_hev"][cx.ctr["hev"] % 2]
                cx.step("vector", lambda e, ww=w, ii=ib, bk=ubank, v=hv: e.tensor_tensor(
                    out=v[0:ii, 0:ww], in0=t["sb_sil"][0:ii, 0:ww], in1=bk[0:ii, 0:ww],
                    op=ALU.mult), [("hev", 1)])
                cx.wait("sync", "hev")
                cx.step("sync", lambda e, a2=i0 + it * 128, b2=tt * 512, ww=w, ii=ib, v=hv: e.dma_start(
                    out=t["h_d"][a2:a2 + ii, b2:b2 + ww], in_=v[0:ii, 0:ww]), [("hst", 16)])
    wdh = t["mk_bf"](0, NI * 1024).rearrange("p (ni c2) -> p ni c2", ni=NI)
    hsl = t["mk_bf"](NI * 1024 * 2, NI * 512).rearrange("p (ni tok) -> p ni tok", ni=NI)
    for chalf in range(2):
        cb0 = chalf * 1024
        if chalf == 0:
            cx.wait("sync", "gumm", cx.ctr["gumm"])
            cx.wait("sync", "hev", cx.ctr["hev"])
        else:
            cx.wait("sync", "dmm", cx.ctr["dmm"])
            cx.wait("sync", "z2st", cx.ctr["z2st"])
        cx.step("sync", lambda e, b=cb0: e.dma_start(
            out=wdh[:, :, :],
            in_=t["wdT_d"].rearrange("(ni p) c2 -> p ni c2", p=128)[:, :, b:b + 1024]),
            [("wdld", 16)])
        for ttg in range(ntt):
            a = ttg * 512
            w = min(512, n - a)
            cx.wait("sync", "dmm", cx.ctr["dmm"])
            cx.step("sync", lambda e, aa=a, ww=w: e.dma_start(
                out=hsl[:, :, 0:ww],
                in_=t["h_d"].rearrange("(ni p) tok -> p ni tok", p=128)[:, :, aa:aa + ww]),
                [("hld", 16)])
            cx.wait("tensor", "wdld", cx.ctr["wdld"])
            cx.wait("tensor", "hld")
            for tk in range(cdiv(w, 128)):
                wk = min(128, w - tk * 128)
                par2 = (cx.ctr["dmm"] // 2) % 2
                zq = 0
                cx.wait("sync", "oev", max(0, cx.ctr["oev"] - 1))
                cx.step("sync", lambda e, aa=a + tk * 128, wkk=wk, b=cb0: e.dma_start(
                    out=t["sb_res"][0:wkk, :], in_=t["x1tm_d"][aa:aa + wkk, b:b + 1024]),
                    [("hld", 16)])
                for ct in range(2):
                    bank = t["ps"][4 + 2 * par2 + ct]
                    cx.wait("tensor", "oev", max(0, cx.ctr["oev"] - 2))
                    cx.wait("tensor", "hld", cx.ctr["hld"])
                    for ic in range(NI):
                        cx.step("tensor", lambda e, i2=ic, k=tk, wkk=wk, c2=ct, bk=bank: e.matmul(
                            bk[0:wkk, 0:512], hsl[:, i2, k * 128:k * 128 + wkk],
                            wdh[:, i2, c2 * 512:(c2 + 1) * 512],
                            start=(i2 == 0), stop=(i2 == NI - 1)),
                            [("dmm", 1)] if ic == NI - 1 else ())
                cx.wait("vector", "dmm")
                if chalf == 0:
                    cx.wait("vector", "z2st", cx.ctr["z2st"])
                    for ct in range(2):
                        cx.step("vector", lambda e, wkk=wk, c2=ct, q=zq, p2=par2: e.tensor_tensor(
                            out=t["sb_z2f"][q][0:wkk, c2 * 512:(c2 + 1) * 512],
                            in0=t["ps"][4 + 2 * p2 + c2][0:wkk, 0:512],
                            in1=t["sb_res"][0:wkk, c2 * 512:(c2 + 1) * 512],
                            op=ALU.add), [("oev", 1)] if ct == 1 else ())
                    cx.wait("sync", "oev")
                    cx.step("sync", lambda e, aa=a + tk * 128, wkk=wk, q=zq: e.dma_start(
                        out=t["z2_d"][aa:aa + wkk, :], in_=t["sb_z2f"][q][0:wkk, 0:1024]),
                        [("z2st", 16)])
                else:
                    cx.wait("sync", "ofin", cx.ctr["ofin"])
                    cx.step("sync", lambda e, aa=a + tk * 128, wkk=wk, q=zq: e.dma_start(
                        out=t["sb_z2f"][q][0:wkk, 0:1024], in_=t["z2_d"][aa:aa + wkk, :]),
                        [("z2ld", 16)])
                    for ct in range(2):
                        cx.step("vector", lambda e, wkk=wk, c2=ct, q=zq, p2=par2: e.tensor_tensor(
                            out=t["sb_z2f"][q][0:wkk, 1024 + c2 * 512:1024 + (c2 + 1) * 512],
                            in0=t["ps"][4 + 2 * p2 + c2][0:wkk, 0:512],
                            in1=t["sb_res"][0:wkk, c2 * 512:(c2 + 1) * 512],
                            op=ALU.add), [("oev", 1)] if ct == 1 else ())
                    cx.wait("vector", "z2ld")
                    cx.step("vector", lambda e, wkk=wk, q=zq: e.reduce_sum(
                        t["sb_st1"][0:wkk, 0:1], t["sb_z2f"][q][0:wkk, :], axis=AX.X))
                    cx.drain("vector")
                    cx.step("vector", lambda e, wkk=wk: e.tensor_scalar(
                        out=t["sb_st1"][0:wkk, 0:1], in0=t["sb_st1"][0:wkk, 0:1],
                        scalar1=1.0 / C, scalar2=None, op0=ALU.mult))
                    cx.drain("vector")
                    cx.step("vector", lambda e: e.nop(), [("st1", 1)])
                    cx.wait("scalar", "ost", max(0, cx.ctr["ost"] - 16))
                    cx.wait("scalar", "st1", cx.ctr["st1"])
                    cx.step("scalar", lambda e, wkk=wk, q=zq: e.activation(
                        out=t["sb_of"][0][0:wkk, :], in_=t["sb_z2f"][q][0:wkk, 0:1024],
                        func=AF.Square, accum_out=t["sb_st2"][0:wkk, 0:1]))
                    cx.step("scalar", lambda e, wkk=wk, q=zq: e.activation(
                        out=t["sb_of"][0][0:wkk, :], in_=t["sb_z2f"][q][0:wkk, 1024:2048],
                        func=AF.Square, accum_out=t["sb_st3"][0:wkk, 0:1]), [("sq2", 1)])
                    cx.wait("vector", "sq2")
                    cx.step("vector", lambda e, wkk=wk: e.tensor_tensor(
                        out=t["sb_st2"][0:wkk, 0:1], in0=t["sb_st2"][0:wkk, 0:1],
                        in1=t["sb_st3"][0:wkk, 0:1], op=ALU.add))
                    cx.drain("vector")
                    cx.step("vector", lambda e, wkk=wk: e.tensor_scalar(
                        out=t["sb_st2"][0:wkk, 0:1], in0=t["sb_st2"][0:wkk, 0:1],
                        scalar1=1.0 / C, scalar2=None, op0=ALU.mult))
                    cx.drain("vector")
                    cx.step("vector", lambda e, wkk=wk: e.tensor_tensor(
                        out=t["sb_st3"][0:wkk, 0:1], in0=t["sb_st1"][0:wkk, 0:1],
                        in1=t["sb_st1"][0:wkk, 0:1], op=ALU.mult))
                    cx.drain("vector")
                    cx.step("vector", lambda e, wkk=wk: e.tensor_tensor(
                        out=t["sb_st2"][0:wkk, 0:1], in0=t["sb_st2"][0:wkk, 0:1],
                        in1=t["sb_st3"][0:wkk, 0:1], op=ALU.subtract))
                    cx.drain("vector")
                    cx.step("vector", lambda e: e.nop(), [("st2", 1)])
                    cx.wait("scalar", "st2")
                    cx.step("scalar", lambda e, wkk=wk: e.activation(
                        out=t["sb_st3"][0:wkk, 0:1], in_=t["sb_st2"][0:wkk, 0:1],
                        func=AF.Ln, bias=t["sb_eps"][0:wkk, :], scale=1.0))
                    cx.drain("scalar")
                    cx.step("scalar", lambda e, wkk=wk: e.activation(
                        out=t["sb_st4"][0:wkk, 0:1], in_=t["sb_st3"][0:wkk, 0:1],
                        func=AF.Exp, scale=-0.5, bias=t["sb_zero"][0:wkk, :]))
                    cx.drain("scalar")
                    cx.step("scalar", lambda e: e.nop(), [("ln2", 2)])
                    cx.wait("vector", "ln2")
                    for oh in range(2):
                        ofv = t["sb_of"][oh]
                        cx.wait("vector", "ost", max(0, cx.ctr["ost"] - 32))
                        if DBG_LN2_RAW:
                            cx.step("vector", lambda e, wkk=wk, q=zq, o2=oh, v=ofv: e.tensor_scalar(
                                out=v[0:wkk, :],
                                in0=t["sb_z2f"][q][0:wkk, o2 * 1024:(o2 + 1) * 1024],
                                scalar1=1.0, scalar2=None,
                                op0=ALU.mult), [("ofin", 1)])
                        else:
                            cx.step("vector", lambda e, wkk=wk, q=zq, o2=oh, v=ofv: e.tensor_scalar(
                                out=v[0:wkk, :],
                                in0=t["sb_z2f"][q][0:wkk, o2 * 1024:(o2 + 1) * 1024],
                                scalar1=t["sb_st1"][0:wkk, 0:1], scalar2=t["sb_st4"][0:wkk, 0:1],
                                op0=ALU.subtract, op1=ALU.mult), [("ofin", 1)])
                        cx.wait("sync", "ofin")
                        cx.step("sync", lambda e, aa=a + tk * 128, wkk=wk, o2=oh, v=ofv: e.dma_start(
                            out=t["out"][aa:aa + wkk, o2 * 1024:(o2 + 1) * 1024],
                            in_=v[0:wkk, :]), [("ost", 16)])


PHASES = 99
DUMPS = ()
DBG_NO_RESID = False
DBG_NO_YSC = False
DBG_LN2_RAW = False
DBG_DVE_SQUARE = False
DUMPABLE = {
    "wg_d": ([INTER, C], BF16), "wu_d": ([INTER, C], BF16), "wd_d": ([C, INTER], BF16),
    "xT_d": ([C, T], BF16), "kT_d": ([C, T], BF16), "qT_d": ([C, NMAX], BF16),
    "v_d": ([T, C], BF16), "y_d": ([C, NMAX], BF16), "sums_d": ([H, NMAX], F32),
    "wgT_d": ([C, INTER], BF16), "wdT_d": ([INTER, C], BF16),
    "x1T_d": ([C, NMAX], BF16), "x1tm_d": ([NMAX, C], F32), "h_d": ([INTER, NMAX], BF16),
    "z2_d": ([NMAX, 1024], F32), "st_d": ([4, NMAX], F32),
}


def emit_arm(cx, c):
    beta, j = c // 4, c % 4
    T0, T1 = BOUNDS[j], BOUNDS[j + 1]
    P = T1
    if PHASES >= 1:
        emit_gather(cx, 0)
    if PHASES >= 2:
        emit_xprep(cx, beta, P)
    if PHASES >= 3:
        emit_qkv(cx, beta, T0, T1, P)
    if PHASES >= 1:
        emit_gather(cx, 1)
    if PHASES >= 4:
        emit_attention(cx, beta, T0, T1, P)
    if PHASES >= 5:
        emit_wtrans(cx)
    if PHASES >= 6:
        emit_proj_ln1(cx, beta, T0, T1)
    if PHASES >= 7:
        emit_ffn_ln2(cx, beta, T0, T1)
    for s in ("gst", "xtst", "kst", "vst", "yst", "wtst", "x1bst", "x1st", "ost"):
        cx.wait("sync", s, cx.ctr[s])
    for dn in DUMPS:
        cx.step("sync", lambda e, d=dn: e.dma_start(
            out=cx.t["dump_" + d], in_=cx.t[d]), [("tagst", 16)])
        cx.wait("sync", "tagst")
    cx.step("sync", lambda e, cc2=c: e.dma_start(
        out=cx.t["tag"], in_=cx.t["armvals"][0:1, cc2:cc2 + 1]), [("tagst", 16)])
    cx.wait("sync", "tagst")


SEM_NAMES = [
    "ld0", "pe0", "ve0",
    "gld", "gdone", "gst", "wtld", "wtst",
    "xc", "xtld", "xtst",
    "wsl", "xsl", "kmm", "kev", "kst", "vmm", "vev", "vst",
    "kvld", "qld", "smm", "pexp", "pmsk", "pfree", "ymm", "summ",
    "yev", "smev", "yst",
    "ysl", "rec", "bcm", "ysc", "yscal", "wpld", "pmm", "xblk", "xtr", "xtoe",
    "zf", "zev", "zsq", "stm", "mu", "var", "lnv", "bc2", "bc2e", "x1f", "x1b",
    "x1tr", "x1tmev", "x1st", "x1bst",
    "x1Tld", "wqld", "gumm", "sil", "hev", "hst",
    "wdld", "hld", "dmm", "oev", "z2st", "z2ld", "st1", "sq2", "sq2r", "st2",
    "ln2", "ofin", "ost", "tagst",
]

BFN = 67072
OFF = {
    "idxb0": 0, "idxb1": 5504, "gacc0": 11008, "gacc1": 16512,
    "gtmp0": 22016, "gtmp1": 27520,
    "win": 33024,
    "hev0": 66048, "hev1": 66560,
}


def build_program():
    nc = bass.Bass("TRN2", target_bir_lowering=False)
    t = {}
    t["x"] = nc.dram_tensor("x", [B, T, C], F32, kind="ExternalInput").ap()
    t["wattn"] = nc.dram_tensor("wattn", [C, 3 * C], F32, kind="ExternalInput").ap()
    t["battn"] = nc.dram_tensor("battn", [1, 3 * C], F32, kind="ExternalInput").ap()
    t["wproj"] = nc.dram_tensor("wproj", [C, C], F32, kind="ExternalInput").ap()
    t["bproj"] = nc.dram_tensor("bproj", [1, C], F32, kind="ExternalInput").ap()
    t["lut"] = nc.dram_tensor("lut", [1, 16], F32, kind="ExternalInput").ap()
    t["gidx"] = nc.dram_tensor("gidx", [INTER, C], I32, kind="ExternalInput").ap()
    t["uidx"] = nc.dram_tensor("uidx", [INTER, C], I32, kind="ExternalInput").ap()
    t["didx"] = nc.dram_tensor("didx", [C, INTER], I32, kind="ExternalInput").ap()
    t["ones_f"] = nc.dram_tensor("ones_f", [1, 128], F32, kind="ExternalInput").ap()
    t["ones_w"] = nc.dram_tensor("ones_w", [1, 512], BF16, kind="ExternalInput").ap()
    t["onescol"] = nc.dram_tensor("onescol", [128, 1], BF16, kind="ExternalInput").ap()
    t["identf"] = nc.dram_tensor("identf", [128, 128], F32, kind="ExternalInput").ap()
    t["dmask"] = nc.dram_tensor("dmask", [4, 128, 512], BF16, kind="ExternalInput").ap()
    t["selh"] = nc.dram_tensor("selh", [16, C], BF16, kind="ExternalInput").ap()
    t["mbias"] = nc.dram_tensor("mbias", [128, 1], F32, kind="ExternalInput").ap()
    t["epsv"] = nc.dram_tensor("epsv", [128, 1], F32, kind="ExternalInput").ap()
    t["zerov"] = nc.dram_tensor("zerov", [128, 1], F32, kind="ExternalInput").ap()
    t["armvals"] = nc.dram_tensor("armvals", [1, 8], F32, kind="ExternalInput").ap()
    t["out"] = nc.dram_tensor("out", [NMAX, C], F32, kind="ExternalOutput").ap()
    t["tag"] = nc.dram_tensor("tag", [1, 1], F32, kind="ExternalOutput").ap()
    for dn in DUMPS:
        shp, dt = DUMPABLE[dn]
        t["dump_" + dn] = nc.dram_tensor("dump_" + dn, shp, dt, kind="ExternalOutput").ap()
    for nm, shp, dt in (
        ("xbf_d", [T, C], BF16), ("xT_d", [C, T], BF16), ("kT_d", [C, T], BF16),
        ("qT_d", [C, NMAX], BF16), ("v_d", [T, C], BF16), ("y_d", [C, NMAX], BF16),
        ("sums_d", [H, NMAX], F32), ("x1T_d", [C, NMAX], BF16),
        ("x1tm_d", [NMAX, C], F32), ("wg_d", [INTER, C], BF16),
        ("wu_d", [INTER, C], BF16), ("wd_d", [C, INTER], BF16),
        ("wgT_d", [C, INTER], BF16), ("wuT_d", [C, INTER], BF16),
        ("wdT_d", [INTER, C], BF16), ("h_d", [INTER, NMAX], BF16),
        ("z2_d", [NMAX, 1024], F32), ("st_d", [4, NMAX], F32),
    ):
        t[nm] = nc.dram_tensor(nm, shp, dt).ap()

    from contextlib import ExitStack
    es = ExitStack()
    arena_bf = es.enter_context(nc.sbuf_tensor("arena_bf", [128, BFN], BF16))

    def bfv(off, nelem):
        assert off + nelem <= BFN, (off, nelem)
        return arena_bf[:, off:off + nelem]

    def mk_bf(byte_off, nelem):
        return bfv(byte_off // 2, nelem)

    t["mk_bf"] = mk_bf
    t["sb_idxb"] = [bfv(OFF["idxb0"], INTER), bfv(OFF["idxb1"], INTER)]
    t["sb_gacc"] = [bfv(OFF["gacc0"], INTER), bfv(OFF["gacc1"], INTER)]
    t["sb_gtmp"] = [bfv(OFF["gtmp0"], INTER), bfv(OFF["gtmp1"], INTER)]
    t["sb_hev"] = [bfv(OFF["hev0"], 512), bfv(OFF["hev1"], 512)]
    W0 = OFF["win"]
    t["v_wslab"] = bfv(W0, CC * 1024).rearrange("p (cc d) -> p cc d", cc=CC)
    t["v_xts"] = bfv(W0 + 16384, CC * 512).rearrange("p (cc w) -> p cc w", cc=CC)
    t["v_kev"] = [bfv(W0 + 24576 + i * 512, 512) for i in range(3)]
    t["v_vev"] = [bfv(W0 + 26112 + i * 1024, 1024) for i in range(2)]
    t["v_K"] = bfv(W0, T)
    t["v_V"] = bfv(W0 + 8192, (T // 128) * 128).rearrange("p (ck d) -> p ck d", ck=T // 128)
    t["v_q"] = [bfv(W0 + 16384 + i * 512, 512) for i in range(2)]
    t["v_PT"] = [bfv(W0 + 17408 + i * 512, 512) for i in range(4)]
    t["v_yev"] = [bfv(W0 + 19456 + i * 512, 512) for i in range(2)]
    t["v_wp"] = bfv(W0, CC * 512).rearrange("p (cc d) -> p cc d", cc=CC)
    t["v_ysl"] = bfv(W0 + 8192, H * 512).rearrange("p (hh w) -> p hh w", hh=H)
    t["v_ysc"] = bfv(W0 + 16384, H * 512).rearrange("p (hh w) -> p hh w", hh=H)
    t["v_zb"] = bfv(W0 + 16384, CC * 512).rearrange("p (cc w) -> p cc w", cc=CC)
    t["v_zsq"] = bfv(W0 + 24576, 512)
    t["v_x1bb"] = [bfv(W0 + 25088, 512), bfv(W0 + 26112, 512)]
    t["sb_rec16b"] = bfv(W0 + 25600, 512)

    FN = 10250
    arena_f = es.enter_context(nc.sbuf_tensor("arena_f", [128, FN], F32))
    fo = [0]

    def af(nelem):
        o = fo[0]
        fo[0] += nelem
        assert fo[0] <= FN, fo[0]
        return arena_f[:, o:o + nelem]

    t["sb_smev"] = [af(512), af(512)]
    t["sb_sums16"] = af(512)
    t["sb_rec16"] = af(512)
    t["sb_xblk"] = [af(128), af(128)]
    t["sb_xto"] = af(512)
    t["sb_z"] = af(512)
    _mublk = af(1024)
    t["sb_mub"] = _mublk[:, 0:512]
    t["sb_rstdb"] = _mublk[:, 512:1024]
    t["sb_x1f"] = af(512)
    t["sb_x1tm"] = [af(128), af(128)]
    t["sb_sil"] = af(512)
    t["sb_res"] = af(1024)
    t["sb_z2f"] = [af(2048)]
    t["sb_zsq2"] = af(512)
    t["sb_of"] = [af(1024), _mublk]
    t["sb_st1"] = af(1)
    t["sb_st2"] = af(1)
    t["sb_st3"] = af(1)
    t["sb_st4"] = af(1)

    def sb(name, shape, dt):
        return es.enter_context(nc.sbuf_tensor(name, shape, dt))

    t["sb_lut"] = sb("sb_lut", [1, 16], F32)[:]
    t["sb_ones_f"] = sb("sb_ones_f", [1, 128], F32)[:]
    t["sb_ones_w"] = sb("sb_ones_w", [1, 512], BF16)[:]
    t["sb_onescol"] = sb("sb_onescol", [128, 1], BF16)[:]
    t["sb_identf"] = sb("sb_identf", [128, 128], F32)[:]
    t["sb_dmask"] = sb("sb_dmask", [128, 4, 512], BF16)[:]
    t["sb_selh"] = sb("sb_selh", [16, C], BF16)[:]
    t["sb_mb"] = sb("sb_mb", [128, 1], F32)[:]
    t["sb_eps"] = sb("sb_eps", [128, 1], F32)[:]
    t["sb_zero"] = sb("sb_zero", [128, 1], F32)[:]
    t["sb_battn"] = sb("sb_battn", [1, 3 * C], BF16)[:]
    t["sb_bproj"] = sb("sb_bproj", [1, C], BF16)[:]
    t["sb_lutbc"] = sb("sb_lutbc", [128, 16], F32)[:]
    t["sb_mu"] = sb("sb_mu", [1, 512], F32)[:]
    t["sb_ex2"] = sb("sb_ex2", [1, 512], F32)[:]
    t["sb_mu2"] = sb("sb_mu2", [1, 512], F32)[:]
    t["sb_xtr"] = [sb("sb_xtr0", [128, 512], BF16)[:], sb("sb_xtr1", [128, 512], BF16)[:]]
    t["sb_wt"] = t["sb_xtr"]

    t["ps"] = [es.enter_context(nc.psum_tensor("ps%d" % i, [128, 512], F32))
               for i in range(8)]
    sems = {s: es.enter_context(nc.semaphore("s_" + s)) for s in SEM_NAMES}

    with es, nc.Block() as block:
        arm_map = _ensure_arms()

        def run_pass(E, eng):
            cx0 = Ctx(E, eng, t, sems)
            emit_prologue(cx0)
            for arm in eng.switch_core_id(n=128):
                idx = int(arm)
                if idx in arm_map:
                    emit_arm(Ctx(E, eng, t, sems, ctr=cx0.ctr), arm_map[idx])

        @block.sync
        def _(sync):
            run_pass("sync", sync)

        @block.tensor
        def _(tensor):
            run_pass("tensor", tensor)

        @block.vector
        def _(vector):
            run_pass("vector", vector)

        @block.scalar
        def _(scalar):
            run_pass("scalar", scalar)

        @block.gpsimd
        def _(gpsimd):
            run_pass("gpsimd", gpsimd)

    return nc


_NC_CACHE = {}


def _discover_arms():
    nc = bass.Bass("TRN2", target_bir_lowering=False)
    pidx = nc.dram_tensor("pidx", [1, 128], F32, kind="ExternalInput").ap()
    py = nc.dram_tensor("py", [1, 1], F32, kind="ExternalOutput").ap()
    from contextlib import ExitStack
    es = ExitStack()
    st = es.enter_context(nc.semaphore("st"))
    with es, nc.Block() as block:
        @block.sync
        def _(sync):
            for arm in sync.switch_core_id(n=128):
                k = int(arm)
                sync.dma_start(out=py, in_=pidx[0:1, k:k + 1]).then_inc(st, 16)
            sync.wait_ge(st, 16)
    ar = np.arange(128, dtype=np.float32).reshape(1, 128)
    res = run_bass_kernel_spmd(nc, [{"pidx": ar} for _ in range(8)],
                               core_ids=list(range(8)))
    arm_map = {}
    for r in range(8):
        k = int(round(float(np.asarray(res.results[r]["py"])[0, 0])))
        arm_map[k] = r
    assert len(arm_map) == 8, f"arm discovery failed: {arm_map}"
    return arm_map


_ARM_MAP = None


def _ensure_arms():
    global _ARM_MAP
    if _ARM_MAP is None:
        _ARM_MAP = _discover_arms()
    return _ARM_MAP


def kernel(**inputs):
    x = np.ascontiguousarray(np.asarray(inputs["L0_attn_x"]), dtype=np.float32)
    wattn = np.ascontiguousarray(np.asarray(inputs["L0_attn_W_attn"]), dtype=np.float32)
    battn = np.ascontiguousarray(np.asarray(inputs["L0_attn_b_attn"]), dtype=np.float32).reshape(1, -1)
    wproj = np.ascontiguousarray(np.asarray(inputs["L0_attn_W_proj"]), dtype=np.float32)
    bproj = np.ascontiguousarray(np.asarray(inputs["L0_attn_b_proj"]), dtype=np.float32).reshape(1, -1)
    lut = np.ascontiguousarray(np.asarray(inputs["L0_ffn_lut"]), dtype=np.float32).reshape(1, -1)
    gidx = np.ascontiguousarray(np.asarray(inputs["L0_ffn_gate_idx"]), dtype=np.int32)
    uidx = np.ascontiguousarray(np.asarray(inputs["L0_ffn_up_idx"]), dtype=np.int32)
    didx = np.ascontiguousarray(np.asarray(inputs["L0_ffn_down_idx"]), dtype=np.int32)

    dmask = np.zeros((4, 128, 512), np.float32)
    for d in range(4):
        for tk in range(128):
            dmask[d, tk, min(512, d * 128 + tk):] = 1.0
    selh = np.zeros((16, C), np.float32)
    for hh in range(16):
        selh[hh, hh * 128:(hh + 1) * 128] = 1.0
    im = {
        "x": x, "wattn": wattn, "battn": battn, "wproj": wproj, "bproj": bproj,
        "lut": lut, "gidx": gidx, "uidx": uidx, "didx": didx,
        "ones_f": np.ones((1, 128), np.float32),
        "ones_w": np.ones((1, 512), ml_dtypes.bfloat16),
        "onescol": np.ones((128, 1), ml_dtypes.bfloat16),
        "identf": np.eye(128, dtype=np.float32),
        "dmask": dmask.astype(ml_dtypes.bfloat16),
        "selh": selh.astype(ml_dtypes.bfloat16),
        "mbias": np.full((128, 1), MB, np.float32),
        "epsv": np.full((128, 1), EPS, np.float32),
        "zerov": np.zeros((128, 1), np.float32),
        "armvals": np.arange(1, 9, dtype=np.float32).reshape(1, 8),
    }
    if "nc" not in _NC_CACHE:
        _NC_CACHE["nc"] = build_program()
    nc = _NC_CACHE["nc"]
    res = run_bass_kernel_spmd(nc, [dict(im) for _ in range(8)], core_ids=list(range(8)))

    out = np.zeros((B, T, C), np.float32)
    seen = set()
    for r in range(8):
        tagv = int(round(float(np.asarray(res.results[r]["tag"])[0, 0]))) - 1
        assert 0 <= tagv < 8 and tagv not in seen, f"bad tags: rank {r} -> {tagv} ({seen})"
        seen.add(tagv)
        beta, j = tagv // 4, tagv % 4
        T0, T1 = BOUNDS[j], BOUNDS[j + 1]
        out[beta, T0:T1, :] = np.asarray(res.results[r]["out"])[0:T1 - T0, :]
    return out


_ensure_arms()
